# version 10
# baseline (speedup 1.0000x reference)
import math
import numpy as np
import concourse.bass as bass
import concourse.mybir as mybir
from concourse.bass_utils import run_bass_kernel_spmd

F32 = mybir.dt.float32
BF16 = mybir.dt.bfloat16
I32 = mybir.dt.int32
ALU = mybir.AluOpType
ACTF = mybir.ActivationFunctionType
AX = mybir.AxisListType

T, D, H, HD, II, NT = 2048, 1024, 16, 64, 4096, 16
KB = 1024
LN127 = float(math.log(127.0))
GRP = [3, 3, 3, 3, 3, 1]
GOF = [0, 3, 6, 9, 12, 15]
NMEAN = 1.0 / (4096.0 * 1024.0)


class Sem:
    def __init__(self, nc, name):
        self.h = nc.alloc_semaphore(name)
        self.n = 0

    def inc(self, instr, k=1):
        instr.then_inc(self.h, k)
        self.n += k
        return instr


def build_nc(debug=False):
    nc = bass.Bass("TRN2", target_bir_lowering=False,
                   detect_race_conditions=False)

    x_d = nc.dram_tensor("x", [T, D], F32, kind="ExternalInput").ap()
    wq_d = nc.dram_tensor("wq", [D, D], F32, kind="ExternalInput").ap()
    wk_d = nc.dram_tensor("wk", [D, D], F32, kind="ExternalInput").ap()
    wv_d = nc.dram_tensor("wv", [D, D], F32, kind="ExternalInput").ap()
    g_d = nc.dram_tensor("g", [1, 1], F32, kind="ExternalInput").ap()
    wg_d = nc.dram_tensor("wg", [D, II], F32, kind="ExternalInput").ap()
    wu_d = nc.dram_tensor("wu", [D, II], F32, kind="ExternalInput").ap()
    wd_d = nc.dram_tensor("wd", [II, D], F32, kind="ExternalInput").ap()
    out_d = nc.dram_tensor("out", [T, D], F32, kind="ExternalOutput").ap()

    KIND = "ExternalOutput" if debug else "Internal"
    tg_d = nc.dram_tensor("tern_g", [D, II], BF16, kind=KIND).ap()
    tu_d = nc.dram_tensor("tern_u", [D, II], BF16, kind=KIND).ap()
    td_d = nc.dram_tensor("tern_d", [II, D], BF16, kind=KIND).ap()
    a_d = nc.dram_tensor("a_scr", [T, D], BF16, kind=KIND).ap()
    x1_d = nc.dram_tensor("x1", [T, D], F32, kind=KIND).ap()
    ahT_d = nc.dram_tensor("aqhT", [NT * 128, II], BF16, kind=KIND).ap()
    red_d = nc.dram_tensor("red", [8, 128], F32, kind="Internal").ap()

    dbg = {}
    if debug:
        dbg["qT"] = nc.dram_tensor("dbg_qT", [128, 16384], F32, kind="ExternalOutput").ap()
        dbg["kT"] = nc.dram_tensor("dbg_kT", [128, 16384], F32, kind="ExternalOutput").ap()
        dbg["vh"] = nc.dram_tensor("dbg_vh", [128, 16640], F32, kind="ExternalOutput").ap()
        dbg["a"] = nc.dram_tensor("dbg_a", [T, D], F32, kind="ExternalOutput").ap()
        dbg["x1"] = nc.dram_tensor("dbg_x1", [T, D], F32, kind="ExternalOutput").ap()
        dbg["h"] = nc.dram_tensor("dbg_h", [128, II], F32, kind="ExternalOutput").ap()
        dbg["aqh"] = nc.dram_tensor("dbg_aqh", [128, II], F32, kind="ExternalOutput").ap()
        dbg["xn1"] = nc.dram_tensor("dbg_xn1", [T, D], F32, kind="ExternalOutput").ap()
        dbg["qf"] = nc.dram_tensor("dbg_qf", [128, D], F32, kind="ExternalOutput").ap()
        dbg["ssq"] = nc.dram_tensor("dbg_ssq", [T, 16], F32, kind="ExternalOutput").ap()
        dbg["rn"] = nc.dram_tensor("dbg_rn", [T, 16], F32, kind="ExternalOutput").ap()
        dbg["dnm"] = nc.dram_tensor("dbg_dnm", [128, 256], F32, kind="ExternalOutput").ap()

    AR = nc.alloc_sbuf_tensor("arena", [128, 200 * KB], mybir.dt.uint8).ap()

    def fv(off, cols, p=128):
        return AR[0:p, off:off + 4 * cols].bitcast(F32)

    def bv(off, cols, p=128):
        return AR[0:p, off:off + 2 * cols].bitcast(BF16)

    def ivv(off, cols, p=128):
        return AR[0:p, off:off + 4 * cols].bitcast(I32)

    ones128 = bv(0, 128)
    g_b = fv(256, 1)
    negabsg = fv(320, 1)
    eps_b = fv(384, 1)
    ln127_b = fv(448, 1)
    m_b = {"g": fv(512, 1), "u": fv(576, 1), "d": fv(640, 1)}
    thr = {"g": fv(704, 1), "u": fv(768, 1), "d": fv(832, 1)}
    nthr = {"g": fv(896, 1), "u": fv(960, 1), "d": fv(1024, 1)}
    wsum = {"g": fv(1088, 1), "u": fv(1152, 1), "d": fv(1216, 1)}
    rsum = fv(1280, 1)
    betan = fv(1344, 16)
    dnm = fv(1472, 256)
    st = fv(2560, 16)
    ssq_q = fv(2688, 16)
    ssq_k = fv(2816, 16)
    inv_q = fv(2944, 16)
    inv_k = fv(3072, 16)
    tmp16 = fv(3200, 16)
    tmp16b = fv(3264, 16)
    rn16 = fv(3328, 16)
    amx = fv(3456, 1)
    amc = fv(3520, 1)
    sxv = fv(3584, 1)
    shv = fv(3648, 1)
    kgn = fv(3712, 1)
    kun = fv(3776, 1)
    sc1 = fv(3840, 1)
    sc2 = fv(3904, 1)
    rowb = fv(4096, 128, p=1)
    dtmp = fv(4608, 1, p=1)
    dval = fv(4672, 1, p=1)
    ident = bv(5120, 128)

    WT_O, KT_O, QT_O, VH_O = 8 * KB, 56 * KB, 88 * KB, 120 * KB
    PT_O, OT_O, OTT_O = 153 * KB, 165 * KB, 167 * KB
    XNT_O = 165 * KB
    WGT_O, WUT_O = 8 * KB, 72 * KB
    XN3_O = 136 * KB
    ATI_O = 140 * KB
    A2_O = 144 * KB
    SQG_O = 146 * KB
    GX1_O = 150 * KB
    ZI_O = 140 * KB
    FQ_O = 144 * KB
    T2_O = 148 * KB
    AQN_O = 152 * KB
    AQT_O = 154 * KB
    RG_O = 156 * KB
    HB_O = 164 * KB
    AQHB_O = 181 * KB
    AHT_O = 189 * KB
    WDT_O = 8 * KB
    AHI_O = 72 * KB
    LN4_O = 140 * KB
    OO_O = 144 * KB
    SQD_O = 148 * KB
    XT_O = 173 * KB
    TO_O = 181 * KB
    T1_O = 185 * KB
    SQ_O = 169 * KB
    QF_O = 153 * KB
    KF_O = 157 * KB
    XN_O = 161 * KB
    X2N_O = 163 * KB
    QH_O = 181 * KB
    KH_O = 185 * KB

    wqT, wkT, wvT = bv(WT_O, 8192), bv(WT_O + 16 * KB, 8192), bv(WT_O + 32 * KB, 8192)
    kT, qT = bv(KT_O, 16384), bv(QT_O, 16384)
    vh = bv(VH_O, 16640)
    ptil = bv(PT_O, 6144)
    oT = [bv(OT_O, 512), bv(OT_O + KB, 512)]
    oTT = [bv(OTT_O + 192 * i, 80) for i in range(16)]
    wgT, wuT, wdT = bv(WGT_O, 32768), bv(WUT_O, 32768), bv(WDT_O, 32768)

    psA = nc.alloc_psum_tensor("ps", [128, 4096], F32).ap()

    def PS(bank, n=512, p=128):
        return psA[0:p, 512 * bank:512 * bank + n]

    Sm = lambda name: Sem(nc, name)
    s_wld, s_wtp, s_g, s_cst = Sm("wld"), Sm("wtp"), Sm("sg"), Sm("cst")
    s_x, s_sq, s_st1, s_rst = Sm("sx"), Sm("ssq"), Sm("sst1"), Sm("srst")
    s_xn1, s_xnT, s_proj, s_ev = Sm("sxn1"), Sm("sxnT"), Sm("sproj"), Sm("sev")
    s_nrm, s_qk, s_qkT = Sm("snrm"), Sm("sqk"), Sm("sqkT")
    s_S, s_exp, s_avj = Sm("sS"), Sm("sexp"), Sm("savj")
    s_oT, s_oTT, s_dn, s_a = Sm("soT"), Sm("soTT"), Sm("sdn"), Sm("sa")
    s_wst, s_ws, s_tst, s_red, s_thr = Sm("swst"), Sm("sws"), Sm("stst"), Sm("sred"), Sm("sthr")
    s_ga, s_gx, s_rn, s_as, s_x1, s_x1st = Sm("sga"), Sm("sgx"), Sm("srn"), Sm("sas"), Sm("sx1"), Sm("sx1st")
    s_wgT, s_q1, s_f2, s_f3, s_aqT = Sm("swgT"), Sm("sq1"), Sm("sf2"), Sm("sf3"), Sm("saqT")
    s_gu, s_f1, s_hq, s_hx, s_hs, s_q2 = Sm("sgu"), Sm("sf1"), Sm("shq"), Sm("shx"), Sm("shs"), Sm("sq2")
    s_ahT, s_ahst, s_wdT = Sm("sahT"), Sm("sahst"), Sm("swdT")
    s_ahl, s_dmm, s_d1, s_d2, s_d3, s_d4, s_out = (
        Sm("sahl"), Sm("sdmm"), Sm("sd1"), Sm("sd2"), Sm("sd3"), Sm("sd4"), Sm("sout"))
    s_dbg = Sm("sdbg")
    s_xP = [Sm("sxA"), Sm("sxB")]
    s_gaP = [Sm("sgaA"), Sm("sgaB")]
    s_gxP = [Sm("sgxA"), Sm("sgxB")]
    s_wstP = [Sm("swstA"), Sm("swstB")]
    s_ahlP = [Sm("sahlA"), Sm("sahlB")]
    s_tstP = [Sm("ststA"), Sm("ststB")]
    s_xnTP = [Sm("sxnTA"), Sm("sxnTB")]
    s_wtp0 = Sm("swtp0")
    s_vf = Sm("svf")
    s_id = Sm("sid")
    s_ta, s_ea = Sm("sta"), Sm("sea")
    s_th, s_eh = Sm("sth"), Sm("seh")
    s_pet, s_pev = Sm("spet"), Sm("spev")

    def VF(e, instr):
        s_vf.inc(instr)
        e.wait_ge(s_vf.h, s_vf.n)
        return instr

    def SF(e, instr):
        s_vf.inc(instr)
        e.wait_ge(s_vf.h, s_vf.n)
        return instr

    with nc.Block() as blk:
        @blk.scalar
        def _(e):
            s_g.inc(e.dma_start(out=g_b, in_=g_d.broadcast_to([128, 1])), 16)

        @blk.vector
        def _(e):
            e.memset(ones128, 1.0)
            e.memset(eps_b, 1e-5)
            e.memset(ln127_b, LN127)
            e.memset(vh, 1.0)
            e.memset(wsum["g"], 0.0)
            e.memset(wsum["u"], 0.0)
            e.memset(wsum["d"], 0.0)
            e.wait_ge(s_g.h, 16)
            e.tensor_scalar(out=sc1, in0=g_b, scalar1=-1.0, scalar2=None,
                            op0=ALU.mult)
            s_cst.inc(e.tensor_tensor(out=negabsg, in0=sc1, in1=g_b, op=ALU.min))

        @blk.gpsimd
        def _(e):
            e.memset(ident, 0.0)
            s_id.inc(e.affine_select(
                out=ident, in_=ident, compare_op=ALU.not_equal, fill=1.0,
                base=0, pattern=[[-1, 128]], channel_multiplier=1))
            for wi, wsrc in enumerate([wq_d, wk_d, wv_d]):
                wT = [wqT, wkT, wvT][wi]
                for c in range(8):
                    s_wld.inc(e.dma_start(
                        out=wT[:, c * 1024:(c + 1) * 1024],
                        in_=wsrc[128 * c:128 * c + 128, :]), 16)

    with nc.Block() as blk:
        @blk.gpsimd
        def _(e):
            for t in range(NT):
                if t >= 2:
                    e.wait_ge(s_xn1.h, t - 1)
                s_xP[t % 2].inc(e.dma_start(out=fv(XT_O + (t % 2) * 4 * KB, 1024),
                                            in_=x_d[128 * t:128 * t + 128, :]), 16)
                if debug:
                    e.wait_ge(s_xn1.h, t + 1)
                    s_dbg.inc(e.dma_start(out=dbg["xn1"][128 * t:128 * t + 128, :],
                                          in_=bv(XN_O, 1024)), 16)
                    if t == 0:
                        e.wait_ge(s_ev.h, 1)
                        s_dbg.inc(e.dma_start(out=dbg["qf"], in_=fv(QF_O, 1024)), 16)
                    e.wait_ge(s_ev.h, t + 1)
                    s_dbg.inc(e.dma_start(out=dbg["ssq"][128 * t:128 * t + 128, :],
                                          in_=ssq_q), 16)
                    e.wait_ge(s_qk.h, t + 1)
                    s_dbg.inc(e.dma_start(out=dbg["rn"][128 * t:128 * t + 128, :],
                                          in_=rn16), 16)

        @blk.scalar
        def _(e):
            psAb8 = psA.bitcast(BF16)[:, 6144:7168].rearrange(
                "p (c w) -> p c w", c=8)

            def evac_qk(dst, tp, pet_target):
                e.wait_ge(s_pet.h, pet_target)
                s_pev.inc(e.activation(
                    out=dst.rearrange("p (c w) -> p c w", c=8)[:, :, 128 * tp:
                                                              128 * tp + 128],
                    in_=psAb8, func=ACTF.Copy))

            for t in range(NT):
                xt = fv(XT_O + (t % 2) * 4 * KB, 1024)
                sq = fv(SQ_O, 1024)
                e.wait_ge(s_xP[t % 2].h, 16 * (t // 2 + 1))
                if t >= 1:
                    e.wait_ge(s_ev.h, t)
                s_sq.inc(e.activation(out=sq, in_=xt, func=ACTF.Square,
                                      accum_out=st[:, 1:2]))
                e.wait_ge(s_st1.h, t + 1)
                e.activation(out=st[:, 6:7], in_=st[:, 5:6], func=ACTF.Ln, bias=eps_b)
                s_rst.inc(e.activation(out=st[:, 7:8], in_=st[:, 6:7], func=ACTF.Exp,
                                       scale=-0.5))
                if t >= 1:
                    evac_qk(qT, t - 1, 3 * t - 1)
                    evac_qk(kT, t - 1, 3 * t)
                e.wait_ge(s_pet.h, 3 * t + 1)
                s_pev.inc(e.activation(out=bv(XNT_O, 1024),
                                       in_=psA.bitcast(BF16)[:, 6144:7168],
                                       func=ACTF.Copy))
                e.wait_ge(s_ev.h, t + 1)
                e.activation(out=tmp16, in_=ssq_q, func=ACTF.Ln)
                e.activation(out=inv_q, in_=tmp16, func=ACTF.Exp, scale=-0.5)
                e.activation(out=tmp16b, in_=ssq_k, func=ACTF.Ln)
                s_nrm.inc(e.activation(out=inv_k, in_=tmp16b, func=ACTF.Exp, scale=-0.5))
            evac_qk(qT, NT - 1, 3 * NT - 1)
            evac_qk(kT, NT - 1, 3 * NT)

        @blk.vector
        def _(e):
            for t in range(NT):
                xt = fv(XT_O + (t % 2) * 4 * KB, 1024)
                e.wait_ge(s_xP[t % 2].h, 16 * (t // 2 + 1))
                VF(e, e.reduce_sum(out=st[:, 0:1], in_=xt, axis=AX.X))
                e.wait_ge(s_sq.h, t + 1)
                SF(e, e.tensor_scalar(out=st[:, 2:3], in0=st[:, 0:1],
                                      scalar1=1.0 / D, scalar2=None,
                                      op0=ALU.mult))
                VF(e, e.tensor_scalar(out=st[:, 3:4], in0=st[:, 1:2], scalar1=1.0 / D,
                                      scalar2=None, op0=ALU.mult))
                VF(e, e.tensor_mul(out=st[:, 4:5], in0=st[:, 2:3], in1=st[:, 2:3]))
                s_st1.inc(e.tensor_sub(out=st[:, 5:6], in0=st[:, 3:4], in1=st[:, 4:5]))
                e.wait_ge(s_rst.h, t + 1)
                if t >= 1:
                    e.wait_ge(s_pet.h, 3 * t - 2)
                xn1 = bv(XN_O, 1024)
                s_xn1.inc(e.tensor_scalar(out=xn1, in0=xt, scalar1=st[:, 2:3],
                                          scalar2=st[:, 7:8], op0=ALU.subtract,
                                          op1=ALU.mult))
                e.wait_ge(s_proj.h, t + 1)
                qf, kf, sq = fv(QF_O, 1024), fv(KF_O, 1024), fv(SQ_O, 1024)
                e.tensor_copy(out=qf[:, 0:512], in_=PS(0))
                e.tensor_copy(out=qf[:, 512:1024], in_=PS(1))
                e.tensor_copy(out=kf[:, 0:512], in_=PS(2))
                e.tensor_copy(out=kf[:, 512:1024], in_=PS(3))
                last = None
                for n in range(2):
                    last = e.tensor_copy(
                        out=vh[:, t * 1040 + 520 * n: t * 1040 + 520 * (n + 1)]
                        .rearrange("p (h d) -> p h d", h=8, d=65)[:, :, 0:64],
                        in_=PS(4 + n).rearrange("p (h d) -> p h d", h=8, d=64))
                VF(e, last)
                VF(e, e.tensor_mul(out=sq, in0=qf, in1=qf))
                VF(e, e.reduce_sum(out=ssq_q,
                                   in_=sq.rearrange("p (h d) -> p h d", d=64),
                                   axis=AX.X))
                e.tensor_scalar(out=ssq_q, in0=ssq_q, scalar1=1e-24, scalar2=None,
                                op0=ALU.max)
                VF(e, e.tensor_mul(out=sq, in0=kf, in1=kf))
                VF(e, e.reduce_sum(out=ssq_k,
                                   in_=sq.rearrange("p (h d) -> p h d", d=64),
                                   axis=AX.X))
                s_ev.inc(e.tensor_scalar(out=ssq_k, in0=ssq_k, scalar1=1e-24,
                                         scalar2=None, op0=ALU.max))
                e.wait_ge(s_nrm.h, t + 1)
                e.wait_ge(s_cst.h, 1)
                if t >= 2:
                    e.wait_ge(s_pet.h, 3 * (t - 1))
                if debug:
                    e.wait_ge(s_dbg.h, 16 * (t + 1 + (1 if t >= 0 else 0)))
                SF(e, e.tensor_scalar(out=rn16, in0=inv_q, scalar1=g_b,
                                      scalar2=None, op0=ALU.mult))
                qh = bv(QH_O + (t % 2) * 2 * KB, 1024)
                kh = bv(KH_O + (t % 2) * 2 * KB, 1024)
                for h in range(H):
                    e.tensor_scalar(out=qh[:, 64 * h:64 * h + 64],
                                    in0=qf[:, 64 * h:64 * h + 64],
                                    scalar1=rn16[:, h:h + 1], scalar2=None,
                                    op0=ALU.mult)
                last = None
                for h in range(H):
                    last = e.tensor_scalar(out=kh[:, 64 * h:64 * h + 64],
                                           in0=kf[:, 64 * h:64 * h + 64],
                                           scalar1=inv_k[:, h:h + 1], scalar2=None,
                                           op0=ALU.mult)
                s_qk.inc(last)

        @blk.tensor
        def _(e):
            e.wait_ge(s_wld.h, 16 * 24)
            e.wait_ge(s_id.h, 1)
            psAb = psA.bitcast(BF16)

            def qk_trans(e, tp):
                e.wait_ge(s_qk.h, tp + 1)
                e.wait_ge(s_pev.h, 3 * tp + 1)
                qh_p = bv(QH_O + (tp % 2) * 2 * KB, 1024)
                kh_p = bv(KH_O + (tp % 2) * 2 * KB, 1024)
                last = None
                for c in range(8):
                    last = e.transpose(
                        out=psAb[:, 6144 + 128 * c:6144 + 128 * (c + 1)],
                        in_=qh_p[:, 128 * c:128 * (c + 1)], identity=ident)
                s_pet.inc(last)
                e.wait_ge(s_pev.h, 3 * tp + 2)
                last = None
                for c in range(8):
                    last = e.transpose(
                        out=psAb[:, 6144 + 128 * c:6144 + 128 * (c + 1)],
                        in_=kh_p[:, 128 * c:128 * (c + 1)], identity=ident)
                s_pet.inc(last)

            for t in range(NT):
                if t >= 1:
                    qk_trans(e, t - 1)
                e.wait_ge(s_xn1.h, t + 1)
                if t >= 1:
                    e.wait_ge(s_pev.h, 3 * t)
                xn1 = bv(XN_O, 1024)
                last = None
                for c in range(8):
                    last = e.transpose(
                        out=psAb[:, 6144 + 128 * c:6144 + 128 * (c + 1)],
                        in_=xn1[:, 128 * c:128 * (c + 1)], identity=ident)
                s_pet.inc(last)
                e.wait_ge(s_pev.h, 3 * t + 1)
                if t >= 1:
                    e.wait_ge(s_ev.h, t)
                last = None
                for c in range(8):
                    xn1T_c = bv(XNT_O + 256 * c, 128)
                    for wi, wT in enumerate([wqT, wkT, wvT]):
                        for n in range(2):
                            last = e.matmul(
                                out=PS(2 * wi + n), lhsT=xn1T_c,
                                rhs=wT[:, c * 1024 + 512 * n: c * 1024 + 512 * (n + 1)],
                                start=(c == 0), stop=(c == 7))
                s_proj.inc(last)
            qk_trans(e, NT - 1)

    nc.all_engine_barrier()

    if debug:
        with nc.Block() as blk:
            @blk.gpsimd
            def _(e):
                s_dbg.inc(e.dma_start(out=dbg["qT"], in_=qT), 16)
                s_dbg.inc(e.dma_start(out=dbg["kT"], in_=kT), 16)
                s_dbg.inc(e.dma_start(out=dbg["vh"], in_=vh), 16)
            @blk.sync
            def _(e):
                e.wait_ge(s_dbg.h, 16 * 3)
        nc.all_engine_barrier()

    items_by_unit = [[] for _ in range(32)]

    def add_item(u, it):
        items_by_unit[u].append(it)

    gu_tiles = [("g", r) for r in range(32)] + [("u", r) for r in range(32)]
    for i, (w, r) in enumerate(gu_tiles[:32]):
        add_item(i // 8, ("p1", w, r))
    for i, (w, r) in enumerate(gu_tiles[32:]):
        add_item(4 + i // 8, ("p1", w, r))
    for i in range(32):
        add_item(8 + i // 4, ("p1d", i // 4, i % 4))
    add_item(8, ("dnc1", "g")); add_item(9, ("dnc2", "g"))
    add_item(10, ("dnc1", "u")); add_item(11, ("dnc2", "u"))
    add_item(16, ("dnc1", "d")); add_item(17, ("dnc2", "d"))
    for i, (w, r) in enumerate(gu_tiles[:32]):
        add_item(11 + i // 4, ("p2", w, r))
    for i, (w, r) in enumerate(gu_tiles[32:]):
        add_item(19 + i // 4, ("p2", w, r))
    for i in range(32):
        add_item(min(27 + i // 7, 31), ("p2d", i // 4, i % 4))

    all_items = [it for u in range(32) for it in items_by_unit[u]]
    unit_of_item = [u for u in range(32) for _ in items_by_unit[u]]
    cum_ws = []
    c = 0
    for u in range(32):
        c += len(items_by_unit[u])
        cum_ws.append(c)
    item_load_idx, item_store_idx = {}, {}
    load_gidx, store_gidx = [], []
    for gi, it in enumerate(all_items):
        if it[0] in ("p1", "p2", "p1d", "p2d"):
            item_load_idx[gi] = len(load_gidx)
            load_gidx.append(gi)
            if it[0] in ("p2", "p2d"):
                item_store_idx[gi] = len(store_gidx)
                store_gidx.append(gi)
    n_gu_stores = sum(1 for it in all_items if it[0] == "p2")

    def load_ap(it):
        if it[0] == "p1" or it[0] == "p2":
            w, r = it[1], it[2]
            kc, cc = r // 4, r % 4
            return {"g": wg_d, "u": wu_d}[w][128 * kc:128 * kc + 128,
                                             1024 * cc:1024 * cc + 1024]
        rc = 4 * it[1] + it[2]
        return wd_d[128 * rc:128 * rc + 128, :]

    def store_ap(it):
        if it[0] == "p2":
            w, r = it[1], it[2]
            kc, cc = r // 4, r % 4
            return {"g": tg_d, "u": tu_d}[w][128 * kc:128 * kc + 128,
                                             1024 * cc:1024 * cc + 1024]
        rc = 4 * it[1] + it[2]
        return td_d[128 * rc:128 * rc + 128, :]

    drank = {"g": 0, "u": 1, "d": 2}
    p1_done_ws = {"g": cum_ws[3], "u": cum_ws[7], "d": cum_ws[15]}

    with nc.Block() as blk:
        @blk.tensor
        def _(e):
            e.wait_ge(s_pev.h, 3 * NT)
            for u in range(32):
                hp, ic = u // 4, u % 4

                def av(j, stop):
                    last = None
                    sl = bv(PT_O + 2048 * (j % 6), 1024)
                    for hh in range(2):
                        vcol = j * 1040 + 65 * (2 * hp + hh)
                        last = e.matmul(
                            out=PS(6 + hh, 512, p=65),
                            lhsT=vh[:, vcol:vcol + 65],
                            rhs=sl[:, 512 * hh:512 * hh + 512],
                            start=(j == 0), stop=stop)
                    s_avj.inc(last)

                def score(j):
                    p = j % 3
                    kcol = hp * 2048 + 128 * j
                    qcol = hp * 2048 + 512 * ic
                    s_S.inc(e.matmul(out=psA[:, 1024 * p:1024 * p + 512],
                                     lhsT=kT[0:64, kcol:kcol + 128],
                                     rhs=qT[0:64, qcol:qcol + 512],
                                     start=True, stop=True,
                                     tile_position=(0, 0)))
                    s_S.inc(e.matmul(out=psA[:, 1024 * p + 512:1024 * (p + 1)],
                                     lhsT=kT[64:128, kcol:kcol + 128],
                                     rhs=qT[64:128, qcol:qcol + 512],
                                     start=True, stop=True,
                                     tile_position=(64, 0)))

                for b in range(6):
                    js = list(range(3 * b, min(3 * b + 3, 16)))
                    if b == 0:
                        e.wait_ge(s_oT.h, 2 * u)
                        if u >= 1:
                            e.wait_ge(s_exp.h, 16 * u)
                    else:
                        e.wait_ge(s_exp.h, 16 * u + 3 * b)
                    for j in js:
                        score(j)
                    for j in range(max(0, 3 * b - 3), 3 * b):
                        av(j, stop=False)
                e.wait_ge(s_exp.h, 16 * u + 16)
                av(15, stop=True)

        @blk.scalar
        def _(e):
            for u in range(32):
                for j in range(16):
                    e.wait_ge(s_S.h, 32 * u + 2 * (j + 1))
                    if 16 * u + j - 5 > 0:
                        e.wait_ge(s_avj.h, 16 * u + j - 5)
                    p = j % 3
                    s_exp.inc(e.activation(
                        out=bv(PT_O + 2048 * (j % 6), 1024),
                        in_=psA[:, 1024 * p:1024 * (p + 1)], func=ACTF.Exp))

        @blk.sync
        def _(e):
            for u in range(32):
                e.wait_ge(s_oT.h, 2 * (u + 1))
                if u >= 2:
                    e.wait_ge(s_dn.h, u - 1)
                    e.wait_ge(s_a.h, 128 * (u - 1))
                for hh in range(2):
                    for b in range(4):
                        s_oTT.inc(e.dma_start_transpose(
                            out=oTT[8 * (u % 2) + 4 * hh + b],
                            in_=oT[hh][0:80, 128 * b:128 * b + 128]), 16)

        @blk.vector
        def _(e):
            gidx = 0
            for u in range(32):
                hp, ic = u // 4, u % 4
                e.wait_ge(s_avj.h, 16 * (u + 1))
                s_oT.inc(e.tensor_copy(out=oT[0][0:65, :], in_=PS(6, p=65)))
                s_oT.inc(e.tensor_copy(out=oT[1][0:65, :], in_=PS(7, p=65)))
                e.wait_ge(s_oTT.h, 128 * (u + 1))
                last = None
                for hh in range(2):
                    for b in range(4):
                        tt = ic * 4 + b
                        col = tt * 16 + 2 * hp + hh
                        last = e.tensor_copy(out=dnm[:, col:col + 1],
                                             in_=oTT[8 * (u % 2) + 4 * hh + b][:, 64:65])
                s_dn.inc(last)
                for it in items_by_unit[u]:
                    if it[0] in ("p1", "p1d"):
                        k = item_load_idx[gidx]
                        wt = fv(XT_O + (k % 2) * 4 * KB, 1024)
                        wname = it[1] if it[0] == "p1" else "d"
                        e.wait_ge(s_wstP[k % 2].h, 16 * (k // 2 + 1))
                        VF(e, e.reduce_sum(out=rsum, in_=wt, axis=AX.X,
                                           apply_absolute_value=True))
                        s_ws.inc(e.tensor_add(out=wsum[wname], in0=wsum[wname],
                                              in1=rsum))
                    elif it[0] in ("p2", "p2d"):
                        k = item_load_idx[gidx]
                        sidx = item_store_idx[gidx]
                        wt = fv(XT_O + (k % 2) * 4 * KB, 1024)
                        wname = it[1] if it[0] == "p2" else "d"
                        e.wait_ge(s_wstP[k % 2].h, 16 * (k // 2 + 1))
                        e.wait_ge(s_thr.h, {"g": 1, "u": 2, "d": 3}[wname])
                        if sidx >= 2:
                            e.wait_ge(s_tstP[sidx % 2].h, 16 * (sidx // 2))
                        t1 = fv(T1_O, 1024)
                        to = bv(TO_O + (sidx % 2) * 2 * KB, 1024)
                        e.tensor_scalar(out=t1, in0=wt, scalar1=thr[wname],
                                        scalar2=None, op0=ALU.is_ge)
                        VF(e, e.tensor_scalar(out=to, in0=wt, scalar1=nthr[wname],
                                              scalar2=None, op0=ALU.is_le))
                        s_ws.inc(e.tensor_sub(out=to, in0=t1, in1=to))
                    elif it[0] == "dnc1":
                        rk = drank[it[1]]
                        e.wait_ge(s_red.h, 64 * rk + 32)
                        VF(e, e.reduce_sum(out=dtmp, in_=rowb, axis=AX.X))
                        s_ws.inc(e.tensor_scalar(out=dval, in0=dtmp, scalar1=NMEAN,
                                                 scalar2=1e-5, op0=ALU.mult,
                                                 op1=ALU.max))
                    elif it[0] == "dnc2":
                        w = it[1]
                        rk = drank[w]
                        e.wait_ge(s_red.h, 64 * rk + 64)
                        SF(e, e.tensor_scalar(out=thr[w], in0=m_b[w], scalar1=0.5,
                                              scalar2=None, op0=ALU.mult))
                        SF(e, e.tensor_scalar(out=nthr[w], in0=thr[w], scalar1=-1.0,
                                              scalar2=None, op0=ALU.mult))
                        VF(e, e.tensor_scalar(out=sc1, in0=nthr[w], scalar1=1.0,
                                               scalar2=None, op0=ALU.mult))
                        s_thr.inc(e.tensor_copy(out=sc1, in_=thr[w]))
                        s_ws.inc(e.tensor_copy(out=sc2, in_=thr[w]))
                    gidx += 1

        @blk.gpsimd
        def _(e):
            gidx = 0
            for u in range(32):
                hp, ic = u // 4, u % 4
                e.wait_ge(s_oTT.h, 128 * (u + 1))
                for hh in range(2):
                    for b in range(4):
                        rows = ic * 512 + 128 * b
                        s_a.inc(e.dma_start(
                            out=a_d[rows:rows + 128,
                                    64 * (2 * hp + hh):64 * (2 * hp + hh) + 64],
                            in_=oTT[8 * (u % 2) + 4 * hh + b][:, 0:64]), 16)
                for it in items_by_unit[u]:
                    if it[0] == "dnc1":
                        w = it[1]
                        rk = drank[w]
                        e.wait_ge(s_ws.h, p1_done_ws[w])
                        s_red.inc(e.dma_start(out=red_d[2 * rk:2 * rk + 1, :],
                                              in_=wsum[w]), 16)
                        e.wait_ge(s_red.h, 64 * rk + 16)
                        s_red.inc(e.dma_start(out=rowb,
                                              in_=red_d[2 * rk:2 * rk + 1, :]), 16)
                    elif it[0] == "dnc2":
                        w = it[1]
                        rk = drank[w]
                        e.wait_ge(s_ws.h, cum_ws[u - 1])
                        s_red.inc(e.dma_start(out=red_d[2 * rk + 1:2 * rk + 2, 0:1],
                                              in_=dval), 16)
                        e.wait_ge(s_red.h, 64 * rk + 48)
                        s_red.inc(e.dma_start(
                            out=m_b[w],
                            in_=red_d[2 * rk + 1:2 * rk + 2, 0:1]
                            .broadcast_to([128, 1])), 16)
                    if it[0] in ("p1", "p2", "p1d", "p2d"):
                        k = item_load_idx[gidx]
                        if k >= 2:
                            e.wait_ge(s_ws.h, load_gidx[k - 2] + 1)
                        s_wstP[k % 2].inc(
                            e.dma_start(out=fv(XT_O + (k % 2) * 4 * KB, 1024),
                                        in_=load_ap(it)), 16)
                    if gidx in item_store_idx:
                        si = item_store_idx[gidx]
                        e.wait_ge(s_ws.h, gidx + 1)
                        s_tstP[si % 2].inc(e.dma_start(
                            out=store_ap(it),
                            in_=bv(TO_O + (si % 2) * 2 * KB, 1024)), 16)
                    gidx += 1

    nc.all_engine_barrier()

    b_sq, b_st1, b_rst, b_x = s_sq.n, s_st1.n, s_rst.n, s_x.n
    with nc.Block() as blk:
        @blk.sync
        def _(e):
            ng0 = (n_gu_stores + 1) // 2
            ng1 = n_gu_stores // 2
            e.wait_ge(s_tstP[0].h, 16 * ng0)
            e.wait_ge(s_tstP[1].h, 16 * ng1)
            for wT, src in [(wgT, tg_d), (wuT, tu_d)]:
                for cc in range(8):
                    s_wgT.inc(e.dma_start(
                        out=wT[:, cc * 4096:(cc + 1) * 4096],
                        in_=src[128 * cc:128 * cc + 128, :]), 16)

        @blk.gpsimd
        def _(e):
            e.wait_ge(s_a.h, 128 * 32)
            if debug:
                s_dbg.inc(e.dma_start(out=dbg["dnm"], in_=dnm), 16)
            for t in range(NT):
                if t >= 2:
                    e.wait_ge(s_as.h, t - 1)
                s_gaP[t % 2].inc(e.dma_start(out=bv(ATI_O + (t % 2) * 2 * KB, 1024),
                                             in_=a_d[128 * t:128 * t + 128, :]), 16)
                e.wait_ge(s_x1.h, t + 1)
                s_x1st.inc(e.dma_start(out=x1_d[128 * t:128 * t + 128, :],
                                       in_=fv(GX1_O, 1024)), 16)
                if debug:
                    s_dbg.inc(e.dma_start(out=dbg["x1"][128 * t:128 * t + 128, :],
                                          in_=fv(GX1_O, 1024)), 16)
                    e.wait_ge(s_as.h, t + 1)
                    s_dbg.inc(e.dma_start(out=dbg["a"][128 * t:128 * t + 128, :],
                                          in_=bv(A2_O, 1024)), 16)

        @blk.scalar
        def _(e):
            for t in range(NT):
                if t >= 1:
                    e.wait_ge(s_x1.h, t)
                s_gxP[t % 2].inc(e.dma_start(out=fv(XT_O + (t % 2) * 4 * KB, 1024),
                                             in_=x_d[128 * t:128 * t + 128, :]), 16)
                e.wait_ge(s_gaP[t % 2].h, 16 * (t // 2 + 1))
                e.activation(out=tmp16, in_=dnm[:, 16 * t:16 * t + 16], func=ACTF.Ln)
                s_rn.inc(e.activation(out=rn16, in_=tmp16, func=ACTF.Exp, scale=-1.0))
                e.wait_ge(s_as.h, t + 1)
                a2 = bv(A2_O, 1024)
                sq = fv(SQG_O, 1024)
                s_sq.inc(e.activation(out=sq, in_=a2, func=ACTF.Square,
                                      accum_out=st[:, 1:2]))
                e.wait_ge(s_st1.h, b_st1 + t + 1)
                e.activation(out=st[:, 6:7], in_=st[:, 5:6], func=ACTF.Ln, bias=eps_b)
                s_rst.inc(e.activation(out=st[:, 7:8], in_=st[:, 6:7], func=ACTF.Exp,
                                       scale=-0.5))

        @blk.vector
        def _(e):
            for t in range(NT):
                e.wait_ge(s_rn.h, t + 1)
                if t >= 1:
                    e.wait_ge(s_sq.h, b_sq + t)
                at = bv(ATI_O + (t % 2) * 2 * KB, 1024)
                a2 = bv(A2_O, 1024)
                last = None
                for h in range(H):
                    last = e.tensor_scalar(out=a2[:, 64 * h:64 * h + 64],
                                           in0=at[:, 64 * h:64 * h + 64],
                                           scalar1=rn16[:, h:h + 1], scalar2=None,
                                           op0=ALU.mult)
                s_as.inc(last)
                e.wait_ge(s_as.h, s_as.n)
                VF(e, e.reduce_sum(out=st[:, 0:1], in_=a2, axis=AX.X))
                e.wait_ge(s_sq.h, b_sq + t + 1)
                SF(e, e.tensor_scalar(out=st[:, 2:3], in0=st[:, 0:1],
                                      scalar1=1.0 / D, scalar2=None,
                                      op0=ALU.mult))
                VF(e, e.tensor_scalar(out=st[:, 3:4], in0=st[:, 1:2],
                                      scalar1=1.0 / D, scalar2=None, op0=ALU.mult))
                VF(e, e.tensor_mul(out=st[:, 4:5], in0=st[:, 2:3], in1=st[:, 2:3]))
                s_st1.inc(e.tensor_sub(out=st[:, 5:6], in0=st[:, 3:4], in1=st[:, 4:5]))
                e.wait_ge(s_rst.h, b_rst + t + 1)
                ln2 = fv(XN3_O, 1024)
                VF(e, e.tensor_scalar(out=ln2, in0=a2, scalar1=st[:, 2:3],
                                      scalar2=st[:, 7:8], op0=ALU.subtract,
                                      op1=ALU.mult))
                e.wait_ge(s_gxP[t % 2].h, 16 * (t // 2 + 1))
                if t >= 1:
                    e.wait_ge(s_x1st.h, 16 * t)
                xt = fv(XT_O + (t % 2) * 4 * KB, 1024)
                s_x1.inc(e.tensor_add(out=fv(GX1_O, 1024), in0=xt, in1=ln2))

    nc.all_engine_barrier()

    b_sq, b_st1, b_rst = s_sq.n, s_st1.n, s_rst.n
    bxp = [s_xP[0].n, s_xP[1].n]
    with nc.Block() as blk:
        @blk.gpsimd
        def _(e):
            e.wait_ge(s_x1st.h, 16 * NT)
            for t in range(NT):
                if t >= 2:
                    e.wait_ge(s_f2.h, t - 1)
                s_xP[t % 2].inc(e.dma_start(out=fv(XT_O + (t % 2) * 4 * KB, 1024),
                                            in_=x1_d[128 * t:128 * t + 128, :]), 16)
                e.wait_ge(s_eh.h, 4 * (t + 1))
                s_ahst.inc(e.dma_start(
                    out=ahT_d[128 * t:128 * t + 128, :],
                    in_=bv(AHT_O, 4096)), 16)
                if debug and t == 0:
                    e.wait_ge(s_hx.h, 1)
                    s_dbg.inc(e.dma_start(out=dbg["h"], in_=bv(HB_O, 4096)), 16)
                    e.wait_ge(s_q2.h, 4)
                    s_dbg.inc(e.dma_start(out=dbg["aqh"], in_=bv(AQHB_O, 4096)), 16)

        @blk.scalar
        def _(e):
            for t in range(NT):
                xt = fv(XT_O + (t % 2) * 4 * KB, 1024)
                sq = fv(FQ_O, 1024)
                e.wait_ge(s_xP[t % 2].h, bxp[t % 2] + 16 * (t // 2 + 1))
                if t >= 1:
                    e.wait_ge(s_q2.h, 4 * t)
                s_sq.inc(e.activation(out=sq, in_=xt, func=ACTF.Square,
                                      accum_out=st[:, 1:2]))
                e.wait_ge(s_st1.h, b_st1 + t + 1)
                e.activation(out=st[:, 6:7], in_=st[:, 5:6], func=ACTF.Ln, bias=eps_b)
                s_rst.inc(e.activation(out=st[:, 7:8], in_=st[:, 6:7], func=ACTF.Exp,
                                       scale=-0.5))
                e.wait_ge(s_f2.h, t + 1)
                e.activation(out=sc2, in_=amc, func=ACTF.Ln)
                s_f3.inc(e.activation(out=sxv, in_=sc2, func=ACTF.Exp, scale=-1.0,
                                      bias=ln127_b))
                e.wait_ge(s_ta.h, t + 1)
                s_ea.inc(e.activation(out=bv(AQT_O, 1024),
                                      in_=psA.bitcast(BF16)[:, 4096:5120],
                                      func=ACTF.Copy))
                e.wait_ge(s_q1.h, t + 1)
                for q in range(4):
                    e.wait_ge(s_gu.h, 4 * t + q + 1)
                    for n in range(2):
                        idx = 2 * q + n
                        if idx >= 2:
                            e.wait_ge(s_hq.h, 8 * t + idx - 1)
                        rg = fv(RG_O + (idx % 2) * 2 * KB, 512)
                        rg2 = fv(RG_O + 4 * KB + (idx % 2) * 2 * KB, 512)
                        e.activation(out=rg, in_=PS(n), func=ACTF.Relu, scale=kgn)
                        s_f1.inc(e.activation(out=rg2, in_=rg, func=ACTF.Square))
                e.wait_ge(s_hx.h, t + 1)
                e.activation(out=sc2, in_=amc, func=ACTF.Ln)
                s_hs.inc(e.activation(out=shv, in_=sc2, func=ACTF.Exp, scale=-1.0,
                                      bias=ln127_b))
                for r in range(4):
                    e.wait_ge(s_th.h, 4 * t + r + 1)
                    if r == 0 and t >= 1:
                        e.wait_ge(s_ahst.h, 16 * t)
                    s_eh.inc(e.activation(out=bv(AHT_O + 2048 * r, 1024),
                                          in_=psA.bitcast(BF16)[:, 6144:7168],
                                          func=ACTF.Copy))

        @blk.vector
        def _(e):
            for t in range(NT):
                xt = fv(XT_O + (t % 2) * 4 * KB, 1024)
                e.wait_ge(s_xP[t % 2].h, bxp[t % 2] + 16 * (t // 2 + 1))
                VF(e, e.reduce_sum(out=st[:, 0:1], in_=xt, axis=AX.X))
                e.wait_ge(s_sq.h, b_sq + t + 1)
                SF(e, e.tensor_scalar(out=st[:, 2:3], in0=st[:, 0:1],
                                      scalar1=1.0 / D, scalar2=None,
                                      op0=ALU.mult))
                VF(e, e.tensor_scalar(out=st[:, 3:4], in0=st[:, 1:2],
                                      scalar1=1.0 / D, scalar2=None, op0=ALU.mult))
                VF(e, e.tensor_mul(out=st[:, 4:5], in0=st[:, 2:3], in1=st[:, 2:3]))
                s_st1.inc(e.tensor_sub(out=st[:, 5:6], in0=st[:, 3:4], in1=st[:, 4:5]))
                e.wait_ge(s_rst.h, b_rst + t + 1)
                xn3 = fv(XN3_O, 1024)
                VF(e, e.tensor_scalar(out=xn3, in0=xt, scalar1=st[:, 2:3],
                                      scalar2=st[:, 7:8], op0=ALU.subtract,
                                      op1=ALU.mult))
                VF(e, e.reduce_max(out=amx, in_=xn3, axis=AX.X,
                                   apply_absolute_value=True))
                s_f2.inc(e.tensor_scalar(out=amc, in0=amx, scalar1=1e-5,
                                         scalar2=None, op0=ALU.max))
                e.wait_ge(s_f3.h, t + 1)
                if t >= 1:
                    e.wait_ge(s_ta.h, t)
                zi = ivv(ZI_O, 1024)
                fq = fv(FQ_O, 1024)
                t2 = fv(T2_O, 1024)
                aqn = bv(AQN_O, 1024)
                VF(e, e.tensor_scalar(out=zi, in0=xn3, scalar1=sxv, scalar2=None,
                                      op0=ALU.mult))
                VF(e, e.scalar_tensor_tensor(out=fq, in0=xn3, scalar=sxv, in1=zi,
                                             op0=ALU.mult, op1=ALU.subtract))
                VF(e, e.scalar_tensor_tensor(out=t2, in0=fq, scalar=0.5, in1=zi,
                                             op0=ALU.is_gt, op1=ALU.add))
                e.scalar_tensor_tensor(out=aqn, in0=fq, scalar=-0.5, in1=t2,
                                       op0=ALU.is_lt, op1=ALU.subtract)
                e.scalar_tensor_tensor(out=kgn, in0=amc, scalar=-1.0 / 127.0,
                                       in1=m_b["g"], op0=ALU.mult, op1=ALU.mult)
                s_q1.inc(e.scalar_tensor_tensor(out=kun, in0=amc,
                                                scalar=-1.0 / 127.0,
                                                in1=m_b["u"], op0=ALU.mult,
                                                op1=ALU.mult))
                hb = bv(HB_O, 4096)
                for q in range(4):
                    for n in range(2):
                        idx = 2 * q + n
                        e.wait_ge(s_f1.h, 8 * t + idx + 1)
                        rg2 = fv(RG_O + 4 * KB + (idx % 2) * 2 * KB, 512)
                        s_hq.inc(e.scalar_tensor_tensor(
                            out=hb[:, 512 * idx:512 * (idx + 1)],
                            in0=PS(2 + n), scalar=kun, in1=rg2,
                            op0=ALU.mult, op1=ALU.mult))
                e.wait_ge(s_hq.h, s_hq.n)
                VF(e, e.reduce_max(out=amx, in_=hb, axis=AX.X,
                                   apply_absolute_value=True))
                VF(e, e.tensor_scalar(out=amc, in0=amx, scalar1=1e-5, scalar2=None,
                                      op0=ALU.max))
                s_hx.inc(e.scalar_tensor_tensor(out=betan[:, t:t + 1], in0=amc,
                                                scalar=-1.0 / 127.0, in1=m_b["d"],
                                                op0=ALU.mult, op1=ALU.mult))
                e.wait_ge(s_hs.h, t + 1)
                if t >= 1:
                    e.wait_ge(s_th.h, 4 * t)
                aqh = bv(AQHB_O, 4096)
                for half in range(4):
                    sl = slice(1024 * half, 1024 * (half + 1))
                    zi8 = ivv(ZI_O, 1024)
                    fq8 = fv(FQ_O, 1024)
                    t28 = fv(T2_O, 1024)
                    VF(e, e.tensor_scalar(out=zi8, in0=hb[:, sl], scalar1=shv,
                                          scalar2=None, op0=ALU.mult))
                    VF(e, e.scalar_tensor_tensor(out=fq8, in0=hb[:, sl], scalar=shv,
                                                 in1=zi8, op0=ALU.mult,
                                                 op1=ALU.subtract))
                    VF(e, e.scalar_tensor_tensor(out=t28, in0=fq8, scalar=0.5,
                                                 in1=zi8, op0=ALU.is_gt,
                                                 op1=ALU.add))
                    s_q2.inc(e.scalar_tensor_tensor(out=aqh[:, sl], in0=fq8,
                                                    scalar=-0.5, in1=t28,
                                                    op0=ALU.is_lt,
                                                    op1=ALU.subtract))

        @blk.tensor
        def _(e):
            e.wait_ge(s_wgT.h, 16 * 16)
            e.wait_ge(s_id.h, 1)
            for t in range(NT):
                e.wait_ge(s_q1.h, t + 1)
                if t >= 1:
                    e.wait_ge(s_ea.h, t)
                aqn = bv(AQN_O, 1024)
                psAb = psA.bitcast(BF16)
                last = None
                for c in range(8):
                    last = e.transpose(
                        out=psAb[:, 4096 + 128 * c:4096 + 128 * (c + 1)],
                        in_=aqn[:, 128 * c:128 * (c + 1)], identity=ident)
                s_ta.inc(last)
                e.wait_ge(s_ea.h, t + 1)
                for q in range(4):
                    if t > 0 or q > 0:
                        e.wait_ge(s_f1.h, 8 * t + 2 * q)
                        e.wait_ge(s_hq.h, 8 * t + 2 * q)
                    last = None
                    for cc in range(8):
                        aqT_c = bv(AQT_O + 256 * cc, 128)
                        for wi, wT in enumerate([wgT, wuT]):
                            for n in range(2):
                                col = cc * 4096 + 1024 * q + 512 * n
                                last = e.matmul(out=PS(2 * wi + n), lhsT=aqT_c,
                                                rhs=wT[:, col:col + 512],
                                                start=(cc == 0), stop=(cc == 7))
                    s_gu.inc(last)
                aqh_t = bv(AQHB_O, 4096)
                for r in range(4):
                    e.wait_ge(s_q2.h, 4 * t + r + 1)
                    if t > 0 or r > 0:
                        e.wait_ge(s_eh.h, 4 * t + r)
                    last = None
                    for c in range(8):
                        last = e.transpose(
                            out=psAb[:, 6144 + 128 * c:6144 + 128 * (c + 1)],
                            in_=aqh_t[:, 1024 * r + 128 * c:1024 * r + 128 * (c + 1)],
                            identity=ident)
                    s_th.inc(last)

    nc.all_engine_barrier()

    with nc.Block() as blk:
        @blk.sync
        def _(e):
            e.wait_ge(s_tstP[0].h, 16 * 48)
            e.wait_ge(s_tstP[1].h, 16 * 48)
            for kk in range(32):
                s_wdT.inc(e.dma_start(
                    out=wdT[:, kk * 1024:(kk + 1) * 1024],
                    in_=td_d[128 * kk:128 * kk + 128, :]), 16)
            for t in range(NT):
                e.wait_ge(s_d4.h, t + 1)
                s_out.inc(e.dma_start(out=out_d[128 * t:128 * t + 128, :],
                                      in_=fv(OO_O, 1024)), 16)
            e.wait_ge(s_out.h, 16 * NT)
            if debug:
                e.wait_ge(s_dbg.h, s_dbg.n)

        @blk.gpsimd
        def _(e):
            e.wait_ge(s_ahst.h, 16 * NT)
            e.wait_ge(s_x1st.h, 16 * NT)
            for t in range(NT):
                if t >= 2:
                    e.wait_ge(s_dmm.h, t - 1)
                s_ahlP[t % 2].inc(e.dma_start(out=bv(AHI_O + (t % 2) * 8 * KB, 4096),
                                              in_=ahT_d[128 * t:128 * t + 128, :]), 16)
                if t >= 2:
                    e.wait_ge(s_d4.h, t - 1)
                s_xP[t % 2].inc(e.dma_start(out=fv(XT_O + (t % 2) * 4 * KB, 1024),
                                            in_=x1_d[128 * t:128 * t + 128, :]), 16)

        @blk.tensor
        def _(e):
            e.wait_ge(s_wdT.h, 16 * 32)
            for t in range(NT):
                e.wait_ge(s_ahlP[t % 2].h, 16 * (t // 2 + 1))
                if t >= 1:
                    e.wait_ge(s_d1.h, t)
                ahi = bv(AHI_O + (t % 2) * 8 * KB, 4096)
                last = None
                for kk in range(32):
                    for n in range(2):
                        last = e.matmul(out=PS(n), lhsT=ahi[:, 128 * kk:128 * kk + 128],
                                        rhs=wdT[:, kk * 1024 + 512 * n:
                                                kk * 1024 + 512 * n + 512],
                                        start=(kk == 0), stop=(kk == 31))
                s_dmm.inc(last)

        @blk.scalar
        def _(e):
            for t in range(NT):
                e.wait_ge(s_dmm.h, t + 1)
                if t >= 1:
                    e.wait_ge(s_d4.h, t)
                dd = fv(XN3_O, 1024)
                sq = fv(SQD_O, 1024)
                e.activation(out=dd[:, 0:512], in_=PS(0), func=ACTF.Copy,
                             scale=betan[:, t:t + 1])
                e.activation(out=dd[:, 512:1024], in_=PS(1), func=ACTF.Copy,
                             scale=betan[:, t:t + 1])
                s_d1.inc(e.activation(out=sq, in_=dd, func=ACTF.Square,
                                      accum_out=st[:, 1:2]))
                e.wait_ge(s_d2.h, t + 1)
                e.activation(out=st[:, 6:7], in_=st[:, 5:6], func=ACTF.Ln, bias=eps_b)
                s_d3.inc(e.activation(out=st[:, 7:8], in_=st[:, 6:7], func=ACTF.Exp,
                                      scale=-0.5))

        @blk.vector
        def _(e):
            bx2p = [s_xP[0].n - 16 * 8, s_xP[1].n - 16 * 8]
            for t in range(NT):
                dd = fv(XN3_O, 1024)
                e.wait_ge(s_d1.h, t + 1)
                VF(e, e.reduce_sum(out=st[:, 0:1], in_=dd, axis=AX.X))
                SF(e, e.tensor_scalar(out=st[:, 2:3], in0=st[:, 0:1],
                                      scalar1=1.0 / D, scalar2=None,
                                      op0=ALU.mult))
                VF(e, e.tensor_scalar(out=st[:, 3:4], in0=st[:, 1:2],
                                      scalar1=1.0 / D, scalar2=None, op0=ALU.mult))
                VF(e, e.tensor_mul(out=st[:, 4:5], in0=st[:, 2:3], in1=st[:, 2:3]))
                s_d2.inc(e.tensor_sub(out=st[:, 5:6], in0=st[:, 3:4],
                                      in1=st[:, 4:5]))
                e.wait_ge(s_d3.h, t + 1)
                ln4 = fv(LN4_O, 1024)
                VF(e, e.tensor_scalar(out=ln4, in0=dd, scalar1=st[:, 2:3],
                                      scalar2=st[:, 7:8], op0=ALU.subtract,
                                      op1=ALU.mult))
                e.wait_ge(s_xP[t % 2].h, bx2p[t % 2] + 16 * (t // 2 + 1))
                if t >= 1:
                    e.wait_ge(s_out.h, 16 * t)
                xt = fv(XT_O + (t % 2) * 4 * KB, 1024)
                s_d4.inc(e.tensor_add(out=fv(OO_O, 1024), in0=xt, in1=ln4))

    return nc, dbg


def kernel(L0_attn_x, L0_attn_Wq, L0_attn_Wk, L0_attn_Wv, L0_attn_g_scale,
           L0_ffn_w_gate, L0_ffn_w_up, L0_ffn_w_down):
    nc, _ = build_nc(debug=False)
    x = np.ascontiguousarray(np.asarray(L0_attn_x, dtype=np.float32))
    g = np.asarray(L0_attn_g_scale, dtype=np.float32).reshape(1, 1)
    com = {
        "wq": np.ascontiguousarray(np.asarray(L0_attn_Wq, np.float32).T),
        "wk": np.ascontiguousarray(np.asarray(L0_attn_Wk, np.float32).T),
        "wv": np.ascontiguousarray(np.asarray(L0_attn_Wv, np.float32).T),
        "g": g,
        "wg": np.ascontiguousarray(np.asarray(L0_ffn_w_gate, np.float32).T),
        "wu": np.ascontiguousarray(np.asarray(L0_ffn_w_up, np.float32).T),
        "wd": np.ascontiguousarray(np.asarray(L0_ffn_w_down, np.float32).T),
    }
    in_maps = [dict(com, x=np.ascontiguousarray(x[i])) for i in range(8)]
    res = run_bass_kernel_spmd(nc, in_maps, core_ids=list(range(8)))
    return np.stack([np.asarray(r["out"], np.float32) for r in res.results], axis=0)


# version 11
# speedup vs baseline: 1.0831x; 1.0831x over previous
import math
import numpy as np
import concourse.bass as bass
import concourse.mybir as mybir
from concourse.bass_utils import run_bass_kernel_spmd

F32 = mybir.dt.float32
BF16 = mybir.dt.bfloat16
I32 = mybir.dt.int32
ALU = mybir.AluOpType
ACTF = mybir.ActivationFunctionType
AX = mybir.AxisListType

T, D, H, HD, II, NT = 2048, 1024, 16, 64, 4096, 16
KB = 1024
LN127 = float(math.log(127.0))
GRP = [3, 3, 3, 3, 3, 1]
GOF = [0, 3, 6, 9, 12, 15]
NMEAN = 1.0 / (4096.0 * 1024.0)


class Sem:
    def __init__(self, nc, name):
        self.h = nc.alloc_semaphore(name)
        self.n = 0

    def inc(self, instr, k=1):
        instr.then_inc(self.h, k)
        self.n += k
        return instr


def build_nc(debug=False):
    nc = bass.Bass("TRN2", target_bir_lowering=False,
                   detect_race_conditions=False)

    x_d = nc.dram_tensor("x", [T, D], F32, kind="ExternalInput").ap()
    wq_d = nc.dram_tensor("wq", [D, D], F32, kind="ExternalInput").ap()
    wk_d = nc.dram_tensor("wk", [D, D], F32, kind="ExternalInput").ap()
    wv_d = nc.dram_tensor("wv", [D, D], F32, kind="ExternalInput").ap()
    g_d = nc.dram_tensor("g", [1, 1], F32, kind="ExternalInput").ap()
    wg_d = nc.dram_tensor("wg", [D, II], F32, kind="ExternalInput").ap()
    wu_d = nc.dram_tensor("wu", [D, II], F32, kind="ExternalInput").ap()
    wd_d = nc.dram_tensor("wd", [II, D], F32, kind="ExternalInput").ap()
    out_d = nc.dram_tensor("out", [T, D], F32, kind="ExternalOutput").ap()

    KIND = "ExternalOutput" if debug else "Internal"
    tg_d = nc.dram_tensor("tern_g", [D, II], BF16, kind=KIND).ap()
    tu_d = nc.dram_tensor("tern_u", [D, II], BF16, kind=KIND).ap()
    td_d = nc.dram_tensor("tern_d", [II, D], BF16, kind=KIND).ap()
    a_d = nc.dram_tensor("a_scr", [T, D], BF16, kind=KIND).ap()
    x1_d = nc.dram_tensor("x1", [T, D], F32, kind=KIND).ap()
    ahT_d = nc.dram_tensor("aqhT", [NT * 128, II], BF16, kind=KIND).ap()
    red_d = nc.dram_tensor("red", [8, 128], F32, kind="Internal").ap()

    dbg = {}
    if debug:
        dbg["qT"] = nc.dram_tensor("dbg_qT", [128, 16384], F32, kind="ExternalOutput").ap()
        dbg["kT"] = nc.dram_tensor("dbg_kT", [128, 16384], F32, kind="ExternalOutput").ap()
        dbg["vh"] = nc.dram_tensor("dbg_vh", [128, 16640], F32, kind="ExternalOutput").ap()
        dbg["a"] = nc.dram_tensor("dbg_a", [T, D], F32, kind="ExternalOutput").ap()
        dbg["x1"] = nc.dram_tensor("dbg_x1", [T, D], F32, kind="ExternalOutput").ap()
        dbg["h"] = nc.dram_tensor("dbg_h", [128, II], F32, kind="ExternalOutput").ap()
        dbg["aqh"] = nc.dram_tensor("dbg_aqh", [128, II], F32, kind="ExternalOutput").ap()
        dbg["xn1"] = nc.dram_tensor("dbg_xn1", [T, D], F32, kind="ExternalOutput").ap()
        dbg["qf"] = nc.dram_tensor("dbg_qf", [128, D], F32, kind="ExternalOutput").ap()
        dbg["ssq"] = nc.dram_tensor("dbg_ssq", [T, 16], F32, kind="ExternalOutput").ap()
        dbg["rn"] = nc.dram_tensor("dbg_rn", [T, 16], F32, kind="ExternalOutput").ap()
        dbg["dnm"] = nc.dram_tensor("dbg_dnm", [128, 256], F32, kind="ExternalOutput").ap()

    AR = nc.alloc_sbuf_tensor("arena", [128, 200 * KB], mybir.dt.uint8).ap()

    def fv(off, cols, p=128):
        return AR[0:p, off:off + 4 * cols].bitcast(F32)

    def bv(off, cols, p=128):
        return AR[0:p, off:off + 2 * cols].bitcast(BF16)

    def ivv(off, cols, p=128):
        return AR[0:p, off:off + 4 * cols].bitcast(I32)

    ones128 = bv(0, 128)
    g_b = fv(256, 1)
    negabsg = fv(320, 1)
    eps_b = fv(384, 1)
    ln127_b = fv(448, 1)
    m_b = {"g": fv(512, 1), "u": fv(576, 1), "d": fv(640, 1)}
    thr = {"g": fv(704, 1), "u": fv(768, 1), "d": fv(832, 1)}
    nthr = {"g": fv(896, 1), "u": fv(960, 1), "d": fv(1024, 1)}
    wsum = {"g": fv(1088, 1), "u": fv(1152, 1), "d": fv(1216, 1)}
    rsum = fv(1280, 1)
    betan = fv(1344, 16)
    dnm = fv(1472, 256)
    st = fv(2560, 16)
    ssq_q = fv(2688, 16)
    ssq_k = fv(2816, 16)
    inv_q = fv(2944, 16)
    inv_k = fv(3072, 16)
    tmp16 = fv(3200, 16)
    tmp16b = fv(3264, 16)
    rn16 = fv(3328, 16)
    amx = fv(3456, 1)
    amc = fv(3520, 1)
    sxv = fv(3584, 1)
    shv = fv(3648, 1)
    kgn = fv(3712, 1)
    kun = fv(3776, 1)
    sc1 = fv(3840, 1)
    sc2 = fv(3904, 1)
    rowb = fv(4096, 128, p=1)
    dtmp = fv(4608, 1, p=1)
    dval = fv(4672, 1, p=1)
    ident = bv(5120, 128)

    WT_O, KT_O, QT_O, VH_O = 8 * KB, 56 * KB, 88 * KB, 120 * KB
    PT_O, OT_O, OTT_O = 153 * KB, 165 * KB, 167 * KB
    XNT_O = 165 * KB
    WGT_O, WUT_O = 8 * KB, 72 * KB
    XN3_O = 136 * KB
    ATI_O = 140 * KB
    A2_O = 144 * KB
    SQG_O = 146 * KB
    GX1_O = 150 * KB
    ZI_O = 140 * KB
    FQ_O = 144 * KB
    T2_O = 148 * KB
    AQN_O = 152 * KB
    AQT_O = 154 * KB
    RG_O = 156 * KB
    HB_O = 164 * KB
    AQHB_O = 181 * KB
    AHT_O = 189 * KB
    WDT_O = 8 * KB
    AHI_O = 72 * KB
    LN4_O = 140 * KB
    OO_O = 144 * KB
    SQD_O = 148 * KB
    XT_O = 173 * KB
    TO_O = 181 * KB
    T1_O = 185 * KB
    SQ_O = 169 * KB
    QF_O = 153 * KB
    KF_O = 157 * KB
    XN_O = 161 * KB
    X2N_O = 163 * KB
    QH_O = 181 * KB
    KH_O = 185 * KB

    wqT, wkT, wvT = bv(WT_O, 8192), bv(WT_O + 16 * KB, 8192), bv(WT_O + 32 * KB, 8192)
    kT, qT = bv(KT_O, 16384), bv(QT_O, 16384)
    vh = bv(VH_O, 16640)
    ptil = bv(PT_O, 6144)
    oT = [bv(OT_O, 512), bv(OT_O + KB, 512)]
    oTT = [bv(OTT_O + 192 * i, 80) for i in range(16)]
    wgT, wuT, wdT = bv(WGT_O, 32768), bv(WUT_O, 32768), bv(WDT_O, 32768)

    psA = nc.alloc_psum_tensor("ps", [128, 4096], F32).ap()

    def PS(bank, n=512, p=128):
        return psA[0:p, 512 * bank:512 * bank + n]

    Sm = lambda name: Sem(nc, name)
    s_wld, s_wtp, s_g, s_cst = Sm("wld"), Sm("wtp"), Sm("sg"), Sm("cst")
    s_x, s_sq, s_st1, s_rst = Sm("sx"), Sm("ssq"), Sm("sst1"), Sm("srst")
    s_xn1, s_xnT, s_proj, s_ev = Sm("sxn1"), Sm("sxnT"), Sm("sproj"), Sm("sev")
    s_nrm, s_qk, s_qkT = Sm("snrm"), Sm("sqk"), Sm("sqkT")
    s_S, s_exp, s_avj = Sm("sS"), Sm("sexp"), Sm("savj")
    s_oT, s_oTT, s_dn, s_a = Sm("soT"), Sm("soTT"), Sm("sdn"), Sm("sa")
    s_wst, s_ws, s_tst, s_red, s_thr = Sm("swst"), Sm("sws"), Sm("stst"), Sm("sred"), Sm("sthr")
    s_ga, s_gx, s_rn, s_as, s_x1, s_x1st = Sm("sga"), Sm("sgx"), Sm("srn"), Sm("sas"), Sm("sx1"), Sm("sx1st")
    s_wgT, s_q1, s_f2, s_f3, s_aqT = Sm("swgT"), Sm("sq1"), Sm("sf2"), Sm("sf3"), Sm("saqT")
    s_gu, s_f1, s_hq, s_hx, s_hs, s_q2 = Sm("sgu"), Sm("sf1"), Sm("shq"), Sm("shx"), Sm("shs"), Sm("sq2")
    s_ahT, s_ahst, s_wdT = Sm("sahT"), Sm("sahst"), Sm("swdT")
    s_ahl, s_dmm, s_d1, s_d2, s_d3, s_d4, s_out = (
        Sm("sahl"), Sm("sdmm"), Sm("sd1"), Sm("sd2"), Sm("sd3"), Sm("sd4"), Sm("sout"))
    s_dbg = Sm("sdbg")
    s_xP = [Sm("sxA"), Sm("sxB")]
    s_gaP = [Sm("sgaA"), Sm("sgaB")]
    s_gxP = [Sm("sgxA"), Sm("sgxB")]
    s_wstP = [Sm("swstA"), Sm("swstB")]
    s_ahlP = [Sm("sahlA"), Sm("sahlB")]
    s_tstP = [Sm("ststA"), Sm("ststB")]
    s_xnTP = [Sm("sxnTA"), Sm("sxnTB")]
    s_wtp0 = Sm("swtp0")
    s_vf = Sm("svf")
    s_id = Sm("sid")
    s_ta, s_ea = Sm("sta"), Sm("sea")
    s_th, s_eh = Sm("sth"), Sm("seh")
    s_pet, s_pev = Sm("spet"), Sm("spev")

    def VF(e, instr):
        s_vf.inc(instr)
        e.wait_ge(s_vf.h, s_vf.n)
        return instr

    def SF(e, instr):
        s_vf.inc(instr)
        e.wait_ge(s_vf.h, s_vf.n)
        return instr

    with nc.Block() as blk:
        @blk.scalar
        def _(e):
            s_g.inc(e.dma_start(out=g_b, in_=g_d.broadcast_to([128, 1])), 16)

        @blk.vector
        def _(e):
            e.memset(ones128, 1.0)
            e.memset(eps_b, 1e-5)
            e.memset(ln127_b, LN127)
            e.memset(vh, 1.0)
            e.memset(wsum["g"], 0.0)
            e.memset(wsum["u"], 0.0)
            e.memset(wsum["d"], 0.0)
            e.wait_ge(s_g.h, 16)
            e.tensor_scalar(out=sc1, in0=g_b, scalar1=-1.0, scalar2=None,
                            op0=ALU.mult)
            s_cst.inc(e.tensor_tensor(out=negabsg, in0=sc1, in1=g_b, op=ALU.min))

        @blk.gpsimd
        def _(e):
            e.memset(ident, 0.0)
            s_id.inc(e.affine_select(
                out=ident, in_=ident, compare_op=ALU.not_equal, fill=1.0,
                base=0, pattern=[[-1, 128]], channel_multiplier=1))
            for wi, wsrc in enumerate([wq_d, wk_d, wv_d]):
                wT = [wqT, wkT, wvT][wi]
                for c in range(8):
                    s_wld.inc(e.dma_start(
                        out=wT[:, c * 1024:(c + 1) * 1024],
                        in_=wsrc[128 * c:128 * c + 128, :]), 16)

    with nc.Block() as blk:
        @blk.gpsimd
        def _(e):
            for t in range(NT):
                if t >= 2:
                    e.wait_ge(s_xn1.h, t - 1)
                s_xP[t % 2].inc(e.dma_start(out=fv(XT_O + (t % 2) * 4 * KB, 1024),
                                            in_=x_d[128 * t:128 * t + 128, :]), 16)
                if debug:
                    e.wait_ge(s_xn1.h, t + 1)
                    s_dbg.inc(e.dma_start(out=dbg["xn1"][128 * t:128 * t + 128, :],
                                          in_=bv(XN_O, 1024)), 16)
                    if t == 0:
                        e.wait_ge(s_ev.h, 1)
                        s_dbg.inc(e.dma_start(out=dbg["qf"], in_=fv(QF_O, 1024)), 16)
                    e.wait_ge(s_ev.h, t + 1)
                    s_dbg.inc(e.dma_start(out=dbg["ssq"][128 * t:128 * t + 128, :],
                                          in_=ssq_q), 16)
                    e.wait_ge(s_qk.h, t + 1)
                    s_dbg.inc(e.dma_start(out=dbg["rn"][128 * t:128 * t + 128, :],
                                          in_=rn16), 16)

        @blk.scalar
        def _(e):
            psAb8 = psA.bitcast(BF16)[:, 6144:7168].rearrange(
                "p (c w) -> p c w", c=8)

            def evac_qk(dst, tp, pet_target):
                e.wait_ge(s_pet.h, pet_target)
                s_pev.inc(e.activation(
                    out=dst.rearrange("p (c w) -> p c w", c=8)[:, :, 128 * tp:
                                                              128 * tp + 128],
                    in_=psAb8, func=ACTF.Copy))

            for t in range(NT):
                xt = fv(XT_O + (t % 2) * 4 * KB, 1024)
                sq = fv(SQ_O, 1024)
                e.wait_ge(s_xP[t % 2].h, 16 * (t // 2 + 1))
                if t >= 1:
                    e.wait_ge(s_ev.h, t)
                s_sq.inc(e.activation(out=sq, in_=xt, func=ACTF.Square,
                                      accum_out=st[:, 1:2]))
                e.wait_ge(s_st1.h, t + 1)
                e.activation(out=st[:, 6:7], in_=st[:, 5:6], func=ACTF.Ln, bias=eps_b)
                s_rst.inc(e.activation(out=st[:, 7:8], in_=st[:, 6:7], func=ACTF.Exp,
                                       scale=-0.5))
                if t >= 1:
                    evac_qk(qT, t - 1, 3 * t - 1)
                    evac_qk(kT, t - 1, 3 * t)
                e.wait_ge(s_pet.h, 3 * t + 1)
                s_pev.inc(e.activation(out=bv(XNT_O, 1024),
                                       in_=psA.bitcast(BF16)[:, 6144:7168],
                                       func=ACTF.Copy))
                e.wait_ge(s_ev.h, t + 1)
                e.activation(out=tmp16, in_=ssq_q, func=ACTF.Ln)
                e.activation(out=inv_q, in_=tmp16, func=ACTF.Exp, scale=-0.5)
                e.activation(out=tmp16b, in_=ssq_k, func=ACTF.Ln)
                s_nrm.inc(e.activation(out=inv_k, in_=tmp16b, func=ACTF.Exp, scale=-0.5))
            evac_qk(qT, NT - 1, 3 * NT - 1)
            evac_qk(kT, NT - 1, 3 * NT)

        @blk.vector
        def _(e):
            for t in range(NT):
                xt = fv(XT_O + (t % 2) * 4 * KB, 1024)
                e.wait_ge(s_xP[t % 2].h, 16 * (t // 2 + 1))
                VF(e, e.reduce_sum(out=st[:, 0:1], in_=xt, axis=AX.X))
                e.wait_ge(s_sq.h, t + 1)
                SF(e, e.tensor_scalar(out=st[:, 2:3], in0=st[:, 0:1],
                                      scalar1=1.0 / D, scalar2=None,
                                      op0=ALU.mult))
                VF(e, e.tensor_scalar(out=st[:, 3:4], in0=st[:, 1:2], scalar1=1.0 / D,
                                      scalar2=None, op0=ALU.mult))
                VF(e, e.tensor_mul(out=st[:, 4:5], in0=st[:, 2:3], in1=st[:, 2:3]))
                s_st1.inc(e.tensor_sub(out=st[:, 5:6], in0=st[:, 3:4], in1=st[:, 4:5]))
                e.wait_ge(s_rst.h, t + 1)
                if t >= 1:
                    e.wait_ge(s_pet.h, 3 * t - 2)
                xn1 = bv(XN_O, 1024)
                s_xn1.inc(e.tensor_scalar(out=xn1, in0=xt, scalar1=st[:, 2:3],
                                          scalar2=st[:, 7:8], op0=ALU.subtract,
                                          op1=ALU.mult))
                e.wait_ge(s_proj.h, t + 1)
                qf, kf, sq = fv(QF_O, 1024), fv(KF_O, 1024), fv(SQ_O, 1024)
                e.tensor_copy(out=qf[:, 0:512], in_=PS(0))
                e.tensor_copy(out=qf[:, 512:1024], in_=PS(1))
                e.tensor_copy(out=kf[:, 0:512], in_=PS(2))
                e.tensor_copy(out=kf[:, 512:1024], in_=PS(3))
                last = None
                for n in range(2):
                    last = e.tensor_copy(
                        out=vh[:, t * 1040 + 520 * n: t * 1040 + 520 * (n + 1)]
                        .rearrange("p (h d) -> p h d", h=8, d=65)[:, :, 0:64],
                        in_=PS(4 + n).rearrange("p (h d) -> p h d", h=8, d=64))
                VF(e, last)
                VF(e, e.tensor_mul(out=sq, in0=qf, in1=qf))
                VF(e, e.reduce_sum(out=ssq_q,
                                   in_=sq.rearrange("p (h d) -> p h d", d=64),
                                   axis=AX.X))
                e.tensor_scalar(out=ssq_q, in0=ssq_q, scalar1=1e-24, scalar2=None,
                                op0=ALU.max)
                VF(e, e.tensor_mul(out=sq, in0=kf, in1=kf))
                VF(e, e.reduce_sum(out=ssq_k,
                                   in_=sq.rearrange("p (h d) -> p h d", d=64),
                                   axis=AX.X))
                s_ev.inc(e.tensor_scalar(out=ssq_k, in0=ssq_k, scalar1=1e-24,
                                         scalar2=None, op0=ALU.max))
                e.wait_ge(s_nrm.h, t + 1)
                e.wait_ge(s_cst.h, 1)
                if t >= 2:
                    e.wait_ge(s_pet.h, 3 * (t - 1))
                if debug:
                    e.wait_ge(s_dbg.h, 16 * (t + 1 + (1 if t >= 0 else 0)))
                SF(e, e.tensor_scalar(out=rn16, in0=inv_q, scalar1=g_b,
                                      scalar2=None, op0=ALU.mult))
                qh = bv(QH_O + (t % 2) * 2 * KB, 1024)
                kh = bv(KH_O + (t % 2) * 2 * KB, 1024)
                for h in range(H):
                    e.tensor_scalar(out=qh[:, 64 * h:64 * h + 64],
                                    in0=qf[:, 64 * h:64 * h + 64],
                                    scalar1=rn16[:, h:h + 1], scalar2=None,
                                    op0=ALU.mult)
                last = None
                for h in range(H):
                    last = e.tensor_scalar(out=kh[:, 64 * h:64 * h + 64],
                                           in0=kf[:, 64 * h:64 * h + 64],
                                           scalar1=inv_k[:, h:h + 1], scalar2=None,
                                           op0=ALU.mult)
                s_qk.inc(last)

        @blk.tensor
        def _(e):
            e.wait_ge(s_wld.h, 16 * 24)
            e.wait_ge(s_id.h, 1)
            psAb = psA.bitcast(BF16)

            def qk_trans(e, tp):
                e.wait_ge(s_qk.h, tp + 1)
                e.wait_ge(s_pev.h, 3 * tp + 1)
                qh_p = bv(QH_O + (tp % 2) * 2 * KB, 1024)
                kh_p = bv(KH_O + (tp % 2) * 2 * KB, 1024)
                last = None
                for c in range(8):
                    last = e.transpose(
                        out=psAb[:, 6144 + 128 * c:6144 + 128 * (c + 1)],
                        in_=qh_p[:, 128 * c:128 * (c + 1)], identity=ident)
                s_pet.inc(last)
                e.wait_ge(s_pev.h, 3 * tp + 2)
                last = None
                for c in range(8):
                    last = e.transpose(
                        out=psAb[:, 6144 + 128 * c:6144 + 128 * (c + 1)],
                        in_=kh_p[:, 128 * c:128 * (c + 1)], identity=ident)
                s_pet.inc(last)

            for t in range(NT):
                if t >= 1:
                    qk_trans(e, t - 1)
                e.wait_ge(s_xn1.h, t + 1)
                if t >= 1:
                    e.wait_ge(s_pev.h, 3 * t)
                xn1 = bv(XN_O, 1024)
                last = None
                for c in range(8):
                    last = e.transpose(
                        out=psAb[:, 6144 + 128 * c:6144 + 128 * (c + 1)],
                        in_=xn1[:, 128 * c:128 * (c + 1)], identity=ident)
                s_pet.inc(last)
                e.wait_ge(s_pev.h, 3 * t + 1)
                if t >= 1:
                    e.wait_ge(s_ev.h, t)
                last = None
                for c in range(8):
                    xn1T_c = bv(XNT_O + 256 * c, 128)
                    for wi, wT in enumerate([wqT, wkT, wvT]):
                        for n in range(2):
                            last = e.matmul(
                                out=PS(2 * wi + n), lhsT=xn1T_c,
                                rhs=wT[:, c * 1024 + 512 * n: c * 1024 + 512 * (n + 1)],
                                start=(c == 0), stop=(c == 7))
                s_proj.inc(last)
            qk_trans(e, NT - 1)

    nc.all_engine_barrier()

    if debug:
        with nc.Block() as blk:
            @blk.gpsimd
            def _(e):
                s_dbg.inc(e.dma_start(out=dbg["qT"], in_=qT), 16)
                s_dbg.inc(e.dma_start(out=dbg["kT"], in_=kT), 16)
                s_dbg.inc(e.dma_start(out=dbg["vh"], in_=vh), 16)
            @blk.sync
            def _(e):
                e.wait_ge(s_dbg.h, 16 * 3)
        nc.all_engine_barrier()

    items_by_unit = [[] for _ in range(32)]

    def add_item(u, it):
        items_by_unit[u].append(it)

    gu_tiles = [("g", r) for r in range(32)] + [("u", r) for r in range(32)]
    for i, (w, r) in enumerate(gu_tiles[:32]):
        add_item(i // 8, ("p1", w, r))
    for i, (w, r) in enumerate(gu_tiles[32:]):
        add_item(4 + i // 8, ("p1", w, r))
    for i in range(32):
        add_item(8 + i // 4, ("p1d", i // 4, i % 4))
    add_item(8, ("dnc1", "g")); add_item(9, ("dnc2", "g"))
    add_item(10, ("dnc1", "u")); add_item(11, ("dnc2", "u"))
    add_item(16, ("dnc1", "d")); add_item(17, ("dnc2", "d"))
    for i, (w, r) in enumerate(gu_tiles[:32]):
        add_item(11 + i // 4, ("p2", w, r))
    for i, (w, r) in enumerate(gu_tiles[32:]):
        add_item(19 + i // 4, ("p2", w, r))
    for i in range(32):
        add_item(min(27 + i // 7, 31), ("p2d", i // 4, i % 4))

    all_items = [it for u in range(32) for it in items_by_unit[u]]
    unit_of_item = [u for u in range(32) for _ in items_by_unit[u]]
    cum_ws = []
    c = 0
    for u in range(32):
        c += len(items_by_unit[u])
        cum_ws.append(c)
    item_load_idx, item_store_idx = {}, {}
    load_gidx, store_gidx = [], []
    for gi, it in enumerate(all_items):
        if it[0] in ("p1", "p2", "p1d", "p2d"):
            item_load_idx[gi] = len(load_gidx)
            load_gidx.append(gi)
            if it[0] in ("p2", "p2d"):
                item_store_idx[gi] = len(store_gidx)
                store_gidx.append(gi)
    n_gu_stores = sum(1 for it in all_items if it[0] == "p2")

    def load_ap(it):
        if it[0] == "p1" or it[0] == "p2":
            w, r = it[1], it[2]
            kc, cc = r // 4, r % 4
            return {"g": wg_d, "u": wu_d}[w][128 * kc:128 * kc + 128,
                                             1024 * cc:1024 * cc + 1024]
        rc = 4 * it[1] + it[2]
        return wd_d[128 * rc:128 * rc + 128, :]

    def store_ap(it):
        if it[0] == "p2":
            w, r = it[1], it[2]
            kc, cc = r // 4, r % 4
            return {"g": tg_d, "u": tu_d}[w][128 * kc:128 * kc + 128,
                                             1024 * cc:1024 * cc + 1024]
        rc = 4 * it[1] + it[2]
        return td_d[128 * rc:128 * rc + 128, :]

    drank = {"g": 0, "u": 1, "d": 2}
    p1_done_ws = {"g": cum_ws[3], "u": cum_ws[7], "d": cum_ws[15]}

    with nc.Block() as blk:
        @blk.tensor
        def _(e):
            e.wait_ge(s_pev.h, 3 * NT)
            for u in range(32):
                hp, ic = u // 4, u % 4

                def av(j, stop):
                    last = None
                    sl = bv(PT_O + 2048 * (j % 6), 1024)
                    for hh in range(2):
                        vcol = j * 1040 + 65 * (2 * hp + hh)
                        last = e.matmul(
                            out=PS(6 + hh, 512, p=65),
                            lhsT=vh[:, vcol:vcol + 65],
                            rhs=sl[:, 512 * hh:512 * hh + 512],
                            start=(j == 0), stop=stop)
                    s_avj.inc(last)

                def score(j):
                    p = j % 3
                    kcol = hp * 2048 + 128 * j
                    qcol = hp * 2048 + 512 * ic
                    s_S.inc(e.matmul(out=psA[:, 1024 * p:1024 * p + 512],
                                     lhsT=kT[0:64, kcol:kcol + 128],
                                     rhs=qT[0:64, qcol:qcol + 512],
                                     start=True, stop=True,
                                     tile_position=(0, 0)))
                    s_S.inc(e.matmul(out=psA[:, 1024 * p + 512:1024 * (p + 1)],
                                     lhsT=kT[64:128, kcol:kcol + 128],
                                     rhs=qT[64:128, qcol:qcol + 512],
                                     start=True, stop=True,
                                     tile_position=(64, 0)))

                for b in range(6):
                    js = list(range(3 * b, min(3 * b + 3, 16)))
                    if b == 0:
                        e.wait_ge(s_oT.h, 2 * u)
                        if u >= 1:
                            e.wait_ge(s_exp.h, 16 * u)
                    else:
                        e.wait_ge(s_exp.h, 16 * u + 3 * b)
                    for j in js:
                        score(j)
                    for j in range(max(0, 3 * b - 3), 3 * b):
                        av(j, stop=False)
                e.wait_ge(s_exp.h, 16 * u + 16)
                av(15, stop=True)

        @blk.scalar
        def _(e):
            for u in range(32):
                for j in range(16):
                    e.wait_ge(s_S.h, 32 * u + 2 * (j + 1))
                    if 16 * u + j - 5 > 0:
                        e.wait_ge(s_avj.h, 16 * u + j - 5)
                    p = j % 3
                    s_exp.inc(e.activation(
                        out=bv(PT_O + 2048 * (j % 6), 1024),
                        in_=psA[:, 1024 * p:1024 * (p + 1)], func=ACTF.Exp))

        @blk.sync
        def _(e):
            for u in range(32):
                e.wait_ge(s_oT.h, 2 * (u + 1))
                if u >= 2:
                    e.wait_ge(s_dn.h, u - 1)
                    e.wait_ge(s_a.h, 128 * (u - 1))
                for hh in range(2):
                    for b in range(4):
                        s_oTT.inc(e.dma_start_transpose(
                            out=oTT[8 * (u % 2) + 4 * hh + b],
                            in_=oT[hh][0:80, 128 * b:128 * b + 128]), 16)
                e.wait_ge(s_oTT.h, 128 * (u + 1))
                for hh in range(2):
                    for b in range(4):
                        rows = (u % 4) * 512 + 128 * b
                        s_a.inc(e.dma_start(
                            out=a_d[rows:rows + 128,
                                    64 * (2 * (u // 4) + hh):
                                    64 * (2 * (u // 4) + hh) + 64],
                            in_=oTT[8 * (u % 2) + 4 * hh + b][:, 0:64]), 16)

        @blk.vector
        def _(e):
            gidx = 0
            for u in range(32):
                hp, ic = u // 4, u % 4
                e.wait_ge(s_avj.h, 16 * (u + 1))
                s_oT.inc(e.tensor_copy(out=oT[0][0:65, :], in_=PS(6, p=65)))
                s_oT.inc(e.tensor_copy(out=oT[1][0:65, :], in_=PS(7, p=65)))
                e.wait_ge(s_oTT.h, 128 * (u + 1))
                last = None
                for hh in range(2):
                    for b in range(4):
                        tt = ic * 4 + b
                        col = tt * 16 + 2 * hp + hh
                        last = e.tensor_copy(out=dnm[:, col:col + 1],
                                             in_=oTT[8 * (u % 2) + 4 * hh + b][:, 64:65])
                s_dn.inc(last)
                for it in items_by_unit[u]:
                    if it[0] in ("p1", "p1d"):
                        k = item_load_idx[gidx]
                        wt = fv(XT_O + (k % 2) * 4 * KB, 1024)
                        wname = it[1] if it[0] == "p1" else "d"
                        e.wait_ge(s_wstP[k % 2].h, 16 * (k // 2 + 1))
                        VF(e, e.reduce_sum(out=rsum, in_=wt, axis=AX.X,
                                           apply_absolute_value=True))
                        s_ws.inc(e.tensor_add(out=wsum[wname], in0=wsum[wname],
                                              in1=rsum))
                    elif it[0] in ("p2", "p2d"):
                        k = item_load_idx[gidx]
                        sidx = item_store_idx[gidx]
                        wt = fv(XT_O + (k % 2) * 4 * KB, 1024)
                        wname = it[1] if it[0] == "p2" else "d"
                        e.wait_ge(s_wstP[k % 2].h, 16 * (k // 2 + 1))
                        e.wait_ge(s_thr.h, {"g": 1, "u": 2, "d": 3}[wname])
                        if sidx >= 2:
                            e.wait_ge(s_tstP[sidx % 2].h, 16 * (sidx // 2))
                        t1 = fv(T1_O, 1024)
                        to = bv(TO_O + (sidx % 2) * 2 * KB, 1024)
                        e.tensor_scalar(out=t1, in0=wt, scalar1=thr[wname],
                                        scalar2=None, op0=ALU.is_ge)
                        VF(e, e.tensor_scalar(out=to, in0=wt, scalar1=nthr[wname],
                                              scalar2=None, op0=ALU.is_le))
                        s_ws.inc(e.tensor_sub(out=to, in0=t1, in1=to))
                    elif it[0] == "dnc1":
                        rk = drank[it[1]]
                        e.wait_ge(s_red.h, 64 * rk + 32)
                        VF(e, e.reduce_sum(out=dtmp, in_=rowb, axis=AX.X))
                        s_ws.inc(e.tensor_scalar(out=dval, in0=dtmp, scalar1=NMEAN,
                                                 scalar2=1e-5, op0=ALU.mult,
                                                 op1=ALU.max))
                    elif it[0] == "dnc2":
                        w = it[1]
                        rk = drank[w]
                        e.wait_ge(s_red.h, 64 * rk + 64)
                        SF(e, e.tensor_scalar(out=thr[w], in0=m_b[w], scalar1=0.5,
                                              scalar2=None, op0=ALU.mult))
                        SF(e, e.tensor_scalar(out=nthr[w], in0=thr[w], scalar1=-1.0,
                                              scalar2=None, op0=ALU.mult))
                        VF(e, e.tensor_scalar(out=sc1, in0=nthr[w], scalar1=1.0,
                                               scalar2=None, op0=ALU.mult))
                        s_thr.inc(e.tensor_copy(out=sc1, in_=thr[w]))
                        s_ws.inc(e.tensor_copy(out=sc2, in_=thr[w]))
                    gidx += 1

        @blk.gpsimd
        def _(e):
            gidx = 0
            for u in range(32):
                for it in items_by_unit[u]:
                    if it[0] == "dnc1":
                        w = it[1]
                        rk = drank[w]
                        e.wait_ge(s_ws.h, p1_done_ws[w])
                        s_red.inc(e.dma_start(out=red_d[2 * rk:2 * rk + 1, :],
                                              in_=wsum[w]), 16)
                        e.wait_ge(s_red.h, 64 * rk + 16)
                        s_red.inc(e.dma_start(out=rowb,
                                              in_=red_d[2 * rk:2 * rk + 1, :]), 16)
                    elif it[0] == "dnc2":
                        w = it[1]
                        rk = drank[w]
                        e.wait_ge(s_ws.h, cum_ws[u - 1])
                        s_red.inc(e.dma_start(out=red_d[2 * rk + 1:2 * rk + 2, 0:1],
                                              in_=dval), 16)
                        e.wait_ge(s_red.h, 64 * rk + 48)
                        s_red.inc(e.dma_start(
                            out=m_b[w],
                            in_=red_d[2 * rk + 1:2 * rk + 2, 0:1]
                            .broadcast_to([128, 1])), 16)
                    if it[0] in ("p1", "p2", "p1d", "p2d"):
                        k = item_load_idx[gidx]
                        if k >= 2:
                            e.wait_ge(s_ws.h, load_gidx[k - 2] + 1)
                        s_wstP[k % 2].inc(
                            e.dma_start(out=fv(XT_O + (k % 2) * 4 * KB, 1024),
                                        in_=load_ap(it)), 16)
                    if gidx in item_store_idx:
                        si = item_store_idx[gidx]
                        e.wait_ge(s_ws.h, gidx + 1)
                        s_tstP[si % 2].inc(e.dma_start(
                            out=store_ap(it),
                            in_=bv(TO_O + (si % 2) * 2 * KB, 1024)), 16)
                    gidx += 1

    nc.all_engine_barrier()

    b_sq, b_st1, b_rst, b_x = s_sq.n, s_st1.n, s_rst.n, s_x.n
    with nc.Block() as blk:
        @blk.sync
        def _(e):
            ng0 = (n_gu_stores + 1) // 2
            ng1 = n_gu_stores // 2
            e.wait_ge(s_tstP[0].h, 16 * ng0)
            e.wait_ge(s_tstP[1].h, 16 * ng1)
            for wT, src in [(wgT, tg_d), (wuT, tu_d)]:
                for cc in range(8):
                    s_wgT.inc(e.dma_start(
                        out=wT[:, cc * 4096:(cc + 1) * 4096],
                        in_=src[128 * cc:128 * cc + 128, :]), 16)

        @blk.gpsimd
        def _(e):
            e.wait_ge(s_a.h, 128 * 32)
            if debug:
                s_dbg.inc(e.dma_start(out=dbg["dnm"], in_=dnm), 16)
            for t in range(NT):
                if t >= 2:
                    e.wait_ge(s_as.h, t - 1)
                s_gaP[t % 2].inc(e.dma_start(out=bv(ATI_O + (t % 2) * 2 * KB, 1024),
                                             in_=a_d[128 * t:128 * t + 128, :]), 16)
                e.wait_ge(s_x1.h, t + 1)
                s_x1st.inc(e.dma_start(out=x1_d[128 * t:128 * t + 128, :],
                                       in_=fv(GX1_O, 1024)), 16)
                if debug:
                    s_dbg.inc(e.dma_start(out=dbg["x1"][128 * t:128 * t + 128, :],
                                          in_=fv(GX1_O, 1024)), 16)
                    e.wait_ge(s_as.h, t + 1)
                    s_dbg.inc(e.dma_start(out=dbg["a"][128 * t:128 * t + 128, :],
                                          in_=bv(A2_O, 1024)), 16)

        @blk.scalar
        def _(e):
            for t in range(NT):
                if t >= 1:
                    e.wait_ge(s_x1.h, t)
                s_gxP[t % 2].inc(e.dma_start(out=fv(XT_O + (t % 2) * 4 * KB, 1024),
                                             in_=x_d[128 * t:128 * t + 128, :]), 16)
                e.wait_ge(s_gaP[t % 2].h, 16 * (t // 2 + 1))
                e.activation(out=tmp16, in_=dnm[:, 16 * t:16 * t + 16], func=ACTF.Ln)
                s_rn.inc(e.activation(out=rn16, in_=tmp16, func=ACTF.Exp, scale=-1.0))
                e.wait_ge(s_as.h, t + 1)
                a2 = bv(A2_O, 1024)
                sq = fv(SQG_O, 1024)
                s_sq.inc(e.activation(out=sq, in_=a2, func=ACTF.Square,
                                      accum_out=st[:, 1:2]))
                e.wait_ge(s_st1.h, b_st1 + t + 1)
                e.activation(out=st[:, 6:7], in_=st[:, 5:6], func=ACTF.Ln, bias=eps_b)
                s_rst.inc(e.activation(out=st[:, 7:8], in_=st[:, 6:7], func=ACTF.Exp,
                                       scale=-0.5))

        @blk.vector
        def _(e):
            for t in range(NT):
                e.wait_ge(s_rn.h, t + 1)
                if t >= 1:
                    e.wait_ge(s_sq.h, b_sq + t)
                at = bv(ATI_O + (t % 2) * 2 * KB, 1024)
                a2 = bv(A2_O, 1024)
                last = None
                for h in range(H):
                    last = e.tensor_scalar(out=a2[:, 64 * h:64 * h + 64],
                                           in0=at[:, 64 * h:64 * h + 64],
                                           scalar1=rn16[:, h:h + 1], scalar2=None,
                                           op0=ALU.mult)
                s_as.inc(last)
                e.wait_ge(s_as.h, s_as.n)
                VF(e, e.reduce_sum(out=st[:, 0:1], in_=a2, axis=AX.X))
                e.wait_ge(s_sq.h, b_sq + t + 1)
                SF(e, e.tensor_scalar(out=st[:, 2:3], in0=st[:, 0:1],
                                      scalar1=1.0 / D, scalar2=None,
                                      op0=ALU.mult))
                VF(e, e.tensor_scalar(out=st[:, 3:4], in0=st[:, 1:2],
                                      scalar1=1.0 / D, scalar2=None, op0=ALU.mult))
                VF(e, e.tensor_mul(out=st[:, 4:5], in0=st[:, 2:3], in1=st[:, 2:3]))
                s_st1.inc(e.tensor_sub(out=st[:, 5:6], in0=st[:, 3:4], in1=st[:, 4:5]))
                e.wait_ge(s_rst.h, b_rst + t + 1)
                ln2 = fv(XN3_O, 1024)
                VF(e, e.tensor_scalar(out=ln2, in0=a2, scalar1=st[:, 2:3],
                                      scalar2=st[:, 7:8], op0=ALU.subtract,
                                      op1=ALU.mult))
                e.wait_ge(s_gxP[t % 2].h, 16 * (t // 2 + 1))
                if t >= 1:
                    e.wait_ge(s_x1st.h, 16 * t)
                xt = fv(XT_O + (t % 2) * 4 * KB, 1024)
                s_x1.inc(e.tensor_add(out=fv(GX1_O, 1024), in0=xt, in1=ln2))

    nc.all_engine_barrier()

    b_sq, b_st1, b_rst = s_sq.n, s_st1.n, s_rst.n
    bxp = [s_xP[0].n, s_xP[1].n]
    with nc.Block() as blk:
        @blk.gpsimd
        def _(e):
            e.wait_ge(s_x1st.h, 16 * NT)
            for t in range(NT):
                if t >= 2:
                    e.wait_ge(s_f2.h, t - 1)
                s_xP[t % 2].inc(e.dma_start(out=fv(XT_O + (t % 2) * 4 * KB, 1024),
                                            in_=x1_d[128 * t:128 * t + 128, :]), 16)
                e.wait_ge(s_eh.h, 4 * (t + 1))
                s_ahst.inc(e.dma_start(
                    out=ahT_d[128 * t:128 * t + 128, :],
                    in_=bv(AHT_O, 4096)), 16)
                if debug and t == 0:
                    e.wait_ge(s_hx.h, 1)
                    s_dbg.inc(e.dma_start(out=dbg["h"], in_=bv(HB_O, 4096)), 16)
                    e.wait_ge(s_q2.h, 4)
                    s_dbg.inc(e.dma_start(out=dbg["aqh"], in_=bv(AQHB_O, 4096)), 16)

        @blk.scalar
        def _(e):
            for t in range(NT):
                xt = fv(XT_O + (t % 2) * 4 * KB, 1024)
                sq = fv(FQ_O, 1024)
                e.wait_ge(s_xP[t % 2].h, bxp[t % 2] + 16 * (t // 2 + 1))
                if t >= 1:
                    e.wait_ge(s_q2.h, 4 * t)
                s_sq.inc(e.activation(out=sq, in_=xt, func=ACTF.Square,
                                      accum_out=st[:, 1:2]))
                e.wait_ge(s_st1.h, b_st1 + t + 1)
                e.activation(out=st[:, 6:7], in_=st[:, 5:6], func=ACTF.Ln, bias=eps_b)
                s_rst.inc(e.activation(out=st[:, 7:8], in_=st[:, 6:7], func=ACTF.Exp,
                                       scale=-0.5))
                e.wait_ge(s_f2.h, t + 1)
                e.activation(out=sc2, in_=amc, func=ACTF.Ln)
                s_f3.inc(e.activation(out=sxv, in_=sc2, func=ACTF.Exp, scale=-1.0,
                                      bias=ln127_b))
                e.wait_ge(s_ta.h, t + 1)
                s_ea.inc(e.activation(out=bv(AQT_O, 1024),
                                      in_=psA.bitcast(BF16)[:, 4096:5120],
                                      func=ACTF.Copy))
                e.wait_ge(s_q1.h, t + 1)
                for q in range(4):
                    e.wait_ge(s_gu.h, 4 * t + q + 1)
                    for n in range(2):
                        idx = 2 * q + n
                        if idx >= 2:
                            e.wait_ge(s_hq.h, 8 * t + idx - 1)
                        rg = fv(RG_O + (idx % 2) * 2 * KB, 512)
                        rg2 = fv(RG_O + 4 * KB + (idx % 2) * 2 * KB, 512)
                        e.activation(out=rg, in_=PS(n), func=ACTF.Relu, scale=kgn)
                        s_f1.inc(e.activation(out=rg2, in_=rg, func=ACTF.Square))
                e.wait_ge(s_hx.h, t + 1)
                e.activation(out=sc2, in_=amc, func=ACTF.Ln)
                s_hs.inc(e.activation(out=shv, in_=sc2, func=ACTF.Exp, scale=-1.0,
                                      bias=ln127_b))
                for r in range(4):
                    e.wait_ge(s_th.h, 4 * t + r + 1)
                    if r == 0 and t >= 1:
                        e.wait_ge(s_ahst.h, 16 * t)
                    s_eh.inc(e.activation(out=bv(AHT_O + 2048 * r, 1024),
                                          in_=psA.bitcast(BF16)[:, 6144:7168],
                                          func=ACTF.Copy))

        @blk.vector
        def _(e):
            for t in range(NT):
                xt = fv(XT_O + (t % 2) * 4 * KB, 1024)
                e.wait_ge(s_xP[t % 2].h, bxp[t % 2] + 16 * (t // 2 + 1))
                VF(e, e.reduce_sum(out=st[:, 0:1], in_=xt, axis=AX.X))
                e.wait_ge(s_sq.h, b_sq + t + 1)
                SF(e, e.tensor_scalar(out=st[:, 2:3], in0=st[:, 0:1],
                                      scalar1=1.0 / D, scalar2=None,
                                      op0=ALU.mult))
                VF(e, e.tensor_scalar(out=st[:, 3:4], in0=st[:, 1:2],
                                      scalar1=1.0 / D, scalar2=None, op0=ALU.mult))
                VF(e, e.tensor_mul(out=st[:, 4:5], in0=st[:, 2:3], in1=st[:, 2:3]))
                s_st1.inc(e.tensor_sub(out=st[:, 5:6], in0=st[:, 3:4], in1=st[:, 4:5]))
                e.wait_ge(s_rst.h, b_rst + t + 1)
                xn3 = fv(XN3_O, 1024)
                VF(e, e.tensor_scalar(out=xn3, in0=xt, scalar1=st[:, 2:3],
                                      scalar2=st[:, 7:8], op0=ALU.subtract,
                                      op1=ALU.mult))
                VF(e, e.reduce_max(out=amx, in_=xn3, axis=AX.X,
                                   apply_absolute_value=True))
                s_f2.inc(e.tensor_scalar(out=amc, in0=amx, scalar1=1e-5,
                                         scalar2=None, op0=ALU.max))
                e.wait_ge(s_f3.h, t + 1)
                if t >= 1:
                    e.wait_ge(s_ta.h, t)
                zi = ivv(ZI_O, 1024)
                fq = fv(FQ_O, 1024)
                t2 = fv(T2_O, 1024)
                aqn = bv(AQN_O, 1024)
                VF(e, e.tensor_scalar(out=zi, in0=xn3, scalar1=sxv, scalar2=None,
                                      op0=ALU.mult))
                VF(e, e.scalar_tensor_tensor(out=fq, in0=xn3, scalar=sxv, in1=zi,
                                             op0=ALU.mult, op1=ALU.subtract))
                VF(e, e.scalar_tensor_tensor(out=t2, in0=fq, scalar=0.5, in1=zi,
                                             op0=ALU.is_gt, op1=ALU.add))
                e.scalar_tensor_tensor(out=aqn, in0=fq, scalar=-0.5, in1=t2,
                                       op0=ALU.is_lt, op1=ALU.subtract)
                e.scalar_tensor_tensor(out=kgn, in0=amc, scalar=-1.0 / 127.0,
                                       in1=m_b["g"], op0=ALU.mult, op1=ALU.mult)
                s_q1.inc(e.scalar_tensor_tensor(out=kun, in0=amc,
                                                scalar=-1.0 / 127.0,
                                                in1=m_b["u"], op0=ALU.mult,
                                                op1=ALU.mult))
                hb = bv(HB_O, 4096)
                for q in range(4):
                    for n in range(2):
                        idx = 2 * q + n
                        e.wait_ge(s_f1.h, 8 * t + idx + 1)
                        rg2 = fv(RG_O + 4 * KB + (idx % 2) * 2 * KB, 512)
                        s_hq.inc(e.scalar_tensor_tensor(
                            out=hb[:, 512 * idx:512 * (idx + 1)],
                            in0=PS(2 + n), scalar=kun, in1=rg2,
                            op0=ALU.mult, op1=ALU.mult))
                e.wait_ge(s_hq.h, s_hq.n)
                VF(e, e.reduce_max(out=amx, in_=hb, axis=AX.X,
                                   apply_absolute_value=True))
                VF(e, e.tensor_scalar(out=amc, in0=amx, scalar1=1e-5, scalar2=None,
                                      op0=ALU.max))
                s_hx.inc(e.scalar_tensor_tensor(out=betan[:, t:t + 1], in0=amc,
                                                scalar=-1.0 / 127.0, in1=m_b["d"],
                                                op0=ALU.mult, op1=ALU.mult))
                e.wait_ge(s_hs.h, t + 1)
                if t >= 1:
                    e.wait_ge(s_th.h, 4 * t)
                aqh = bv(AQHB_O, 4096)
                for half in range(4):
                    sl = slice(1024 * half, 1024 * (half + 1))
                    zi8 = ivv(ZI_O, 1024)
                    fq8 = fv(FQ_O, 1024)
                    t28 = fv(T2_O, 1024)
                    VF(e, e.tensor_scalar(out=zi8, in0=hb[:, sl], scalar1=shv,
                                          scalar2=None, op0=ALU.mult))
                    VF(e, e.scalar_tensor_tensor(out=fq8, in0=hb[:, sl], scalar=shv,
                                                 in1=zi8, op0=ALU.mult,
                                                 op1=ALU.subtract))
                    VF(e, e.scalar_tensor_tensor(out=t28, in0=fq8, scalar=0.5,
                                                 in1=zi8, op0=ALU.is_gt,
                                                 op1=ALU.add))
                    s_q2.inc(e.scalar_tensor_tensor(out=aqh[:, sl], in0=fq8,
                                                    scalar=-0.5, in1=t28,
                                                    op0=ALU.is_lt,
                                                    op1=ALU.subtract))

        @blk.tensor
        def _(e):
            e.wait_ge(s_wgT.h, 16 * 16)
            e.wait_ge(s_id.h, 1)
            for t in range(NT):
                e.wait_ge(s_q1.h, t + 1)
                if t >= 1:
                    e.wait_ge(s_ea.h, t)
                aqn = bv(AQN_O, 1024)
                psAb = psA.bitcast(BF16)
                last = None
                for c in range(8):
                    last = e.transpose(
                        out=psAb[:, 4096 + 128 * c:4096 + 128 * (c + 1)],
                        in_=aqn[:, 128 * c:128 * (c + 1)], identity=ident)
                s_ta.inc(last)
                e.wait_ge(s_ea.h, t + 1)
                for q in range(4):
                    if t > 0 or q > 0:
                        e.wait_ge(s_f1.h, 8 * t + 2 * q)
                        e.wait_ge(s_hq.h, 8 * t + 2 * q)
                    last = None
                    for cc in range(8):
                        aqT_c = bv(AQT_O + 256 * cc, 128)
                        for wi, wT in enumerate([wgT, wuT]):
                            for n in range(2):
                                col = cc * 4096 + 1024 * q + 512 * n
                                last = e.matmul(out=PS(2 * wi + n), lhsT=aqT_c,
                                                rhs=wT[:, col:col + 512],
                                                start=(cc == 0), stop=(cc == 7))
                    s_gu.inc(last)
                aqh_t = bv(AQHB_O, 4096)
                for r in range(4):
                    e.wait_ge(s_q2.h, 4 * t + r + 1)
                    if t > 0 or r > 0:
                        e.wait_ge(s_eh.h, 4 * t + r)
                    last = None
                    for c in range(8):
                        last = e.transpose(
                            out=psAb[:, 6144 + 128 * c:6144 + 128 * (c + 1)],
                            in_=aqh_t[:, 1024 * r + 128 * c:1024 * r + 128 * (c + 1)],
                            identity=ident)
                    s_th.inc(last)

    nc.all_engine_barrier()

    with nc.Block() as blk:
        @blk.sync
        def _(e):
            e.wait_ge(s_tstP[0].h, 16 * 48)
            e.wait_ge(s_tstP[1].h, 16 * 48)
            for kk in range(32):
                s_wdT.inc(e.dma_start(
                    out=wdT[:, kk * 1024:(kk + 1) * 1024],
                    in_=td_d[128 * kk:128 * kk + 128, :]), 16)
            for t in range(NT):
                e.wait_ge(s_d4.h, t + 1)
                s_out.inc(e.dma_start(out=out_d[128 * t:128 * t + 128, :],
                                      in_=fv(OO_O, 1024)), 16)
            e.wait_ge(s_out.h, 16 * NT)
            if debug:
                e.wait_ge(s_dbg.h, s_dbg.n)

        @blk.gpsimd
        def _(e):
            e.wait_ge(s_ahst.h, 16 * NT)
            e.wait_ge(s_x1st.h, 16 * NT)
            for t in range(NT):
                if t >= 2:
                    e.wait_ge(s_dmm.h, t - 1)
                s_ahlP[t % 2].inc(e.dma_start(out=bv(AHI_O + (t % 2) * 8 * KB, 4096),
                                              in_=ahT_d[128 * t:128 * t + 128, :]), 16)
                if t >= 2:
                    e.wait_ge(s_d4.h, t - 1)
                s_xP[t % 2].inc(e.dma_start(out=fv(XT_O + (t % 2) * 4 * KB, 1024),
                                            in_=x1_d[128 * t:128 * t + 128, :]), 16)

        @blk.tensor
        def _(e):
            e.wait_ge(s_wdT.h, 16 * 32)
            for t in range(NT):
                e.wait_ge(s_ahlP[t % 2].h, 16 * (t // 2 + 1))
                if t >= 1:
                    e.wait_ge(s_d1.h, t)
                ahi = bv(AHI_O + (t % 2) * 8 * KB, 4096)
                last = None
                for kk in range(32):
                    for n in range(2):
                        last = e.matmul(out=PS(n), lhsT=ahi[:, 128 * kk:128 * kk + 128],
                                        rhs=wdT[:, kk * 1024 + 512 * n:
                                                kk * 1024 + 512 * n + 512],
                                        start=(kk == 0), stop=(kk == 31))
                s_dmm.inc(last)

        @blk.scalar
        def _(e):
            for t in range(NT):
                e.wait_ge(s_dmm.h, t + 1)
                if t >= 1:
                    e.wait_ge(s_d4.h, t)
                dd = fv(XN3_O, 1024)
                sq = fv(SQD_O, 1024)
                e.activation(out=dd[:, 0:512], in_=PS(0), func=ACTF.Copy,
                             scale=betan[:, t:t + 1])
                e.activation(out=dd[:, 512:1024], in_=PS(1), func=ACTF.Copy,
                             scale=betan[:, t:t + 1])
                s_d1.inc(e.activation(out=sq, in_=dd, func=ACTF.Square,
                                      accum_out=st[:, 1:2]))
                e.wait_ge(s_d2.h, t + 1)
                e.activation(out=st[:, 6:7], in_=st[:, 5:6], func=ACTF.Ln, bias=eps_b)
                s_d3.inc(e.activation(out=st[:, 7:8], in_=st[:, 6:7], func=ACTF.Exp,
                                      scale=-0.5))

        @blk.vector
        def _(e):
            bx2p = [s_xP[0].n - 16 * 8, s_xP[1].n - 16 * 8]
            for t in range(NT):
                dd = fv(XN3_O, 1024)
                e.wait_ge(s_d1.h, t + 1)
                VF(e, e.reduce_sum(out=st[:, 0:1], in_=dd, axis=AX.X))
                SF(e, e.tensor_scalar(out=st[:, 2:3], in0=st[:, 0:1],
                                      scalar1=1.0 / D, scalar2=None,
                                      op0=ALU.mult))
                VF(e, e.tensor_scalar(out=st[:, 3:4], in0=st[:, 1:2],
                                      scalar1=1.0 / D, scalar2=None, op0=ALU.mult))
                VF(e, e.tensor_mul(out=st[:, 4:5], in0=st[:, 2:3], in1=st[:, 2:3]))
                s_d2.inc(e.tensor_sub(out=st[:, 5:6], in0=st[:, 3:4],
                                      in1=st[:, 4:5]))
                e.wait_ge(s_d3.h, t + 1)
                ln4 = fv(LN4_O, 1024)
                VF(e, e.tensor_scalar(out=ln4, in0=dd, scalar1=st[:, 2:3],
                                      scalar2=st[:, 7:8], op0=ALU.subtract,
                                      op1=ALU.mult))
                e.wait_ge(s_xP[t % 2].h, bx2p[t % 2] + 16 * (t // 2 + 1))
                if t >= 1:
                    e.wait_ge(s_out.h, 16 * t)
                xt = fv(XT_O + (t % 2) * 4 * KB, 1024)
                s_d4.inc(e.tensor_add(out=fv(OO_O, 1024), in0=xt, in1=ln4))

    return nc, dbg


def kernel(L0_attn_x, L0_attn_Wq, L0_attn_Wk, L0_attn_Wv, L0_attn_g_scale,
           L0_ffn_w_gate, L0_ffn_w_up, L0_ffn_w_down):
    nc, _ = build_nc(debug=False)
    x = np.ascontiguousarray(np.asarray(L0_attn_x, dtype=np.float32))
    g = np.asarray(L0_attn_g_scale, dtype=np.float32).reshape(1, 1)
    com = {
        "wq": np.ascontiguousarray(np.asarray(L0_attn_Wq, np.float32).T),
        "wk": np.ascontiguousarray(np.asarray(L0_attn_Wk, np.float32).T),
        "wv": np.ascontiguousarray(np.asarray(L0_attn_Wv, np.float32).T),
        "g": g,
        "wg": np.ascontiguousarray(np.asarray(L0_ffn_w_gate, np.float32).T),
        "wu": np.ascontiguousarray(np.asarray(L0_ffn_w_up, np.float32).T),
        "wd": np.ascontiguousarray(np.asarray(L0_ffn_w_down, np.float32).T),
    }
    in_maps = [dict(com, x=np.ascontiguousarray(x[i])) for i in range(8)]
    res = run_bass_kernel_spmd(nc, in_maps, core_ids=list(range(8)))
    return np.stack([np.asarray(r["out"], np.float32) for r in res.results], axis=0)


# version 12 (final)
# speedup vs baseline: 1.1509x; 1.0626x over previous
import math
import numpy as np
import concourse.bass as bass
import concourse.mybir as mybir
from concourse.bass_utils import run_bass_kernel_spmd

F32 = mybir.dt.float32
BF16 = mybir.dt.bfloat16
I32 = mybir.dt.int32
ALU = mybir.AluOpType
ACTF = mybir.ActivationFunctionType
AX = mybir.AxisListType

T, D, H, HD, II, NT = 2048, 1024, 16, 64, 4096, 16
KB = 1024
LN127 = float(math.log(127.0))
GRP = [3, 3, 3, 3, 3, 1]
GOF = [0, 3, 6, 9, 12, 15]
NMEAN = 1.0 / (4096.0 * 1024.0)


class Sem:
    def __init__(self, nc, name):
        self.h = nc.alloc_semaphore(name)
        self.n = 0

    def inc(self, instr, k=1):
        instr.then_inc(self.h, k)
        self.n += k
        return instr


def build_nc(debug=False):
    nc = bass.Bass("TRN2", target_bir_lowering=False,
                   detect_race_conditions=False)

    x_d = nc.dram_tensor("x", [T, D], F32, kind="ExternalInput").ap()
    wq_d = nc.dram_tensor("wq", [D, D], F32, kind="ExternalInput").ap()
    wk_d = nc.dram_tensor("wk", [D, D], F32, kind="ExternalInput").ap()
    wv_d = nc.dram_tensor("wv", [D, D], F32, kind="ExternalInput").ap()
    g_d = nc.dram_tensor("g", [1, 1], F32, kind="ExternalInput").ap()
    wg_d = nc.dram_tensor("wg", [D, II], F32, kind="ExternalInput").ap()
    wu_d = nc.dram_tensor("wu", [D, II], F32, kind="ExternalInput").ap()
    wd_d = nc.dram_tensor("wd", [II, D], F32, kind="ExternalInput").ap()
    out_d = nc.dram_tensor("out", [T, D], F32, kind="ExternalOutput").ap()

    KIND = "ExternalOutput" if debug else "Internal"
    tg_d = nc.dram_tensor("tern_g", [D, II], BF16, kind=KIND).ap()
    tu_d = nc.dram_tensor("tern_u", [D, II], BF16, kind=KIND).ap()
    td_d = nc.dram_tensor("tern_d", [II, D], BF16, kind=KIND).ap()
    a_d = nc.dram_tensor("a_scr", [T, D], BF16, kind=KIND).ap()
    x1_d = nc.dram_tensor("x1", [T, D], F32, kind=KIND).ap()
    ahT_d = nc.dram_tensor("aqhT", [NT * 128, II], BF16, kind=KIND).ap()
    red_d = nc.dram_tensor("red", [8, 128], F32, kind="Internal").ap()

    dbg = {}
    if debug:
        dbg["qT"] = nc.dram_tensor("dbg_qT", [128, 16384], F32, kind="ExternalOutput").ap()
        dbg["kT"] = nc.dram_tensor("dbg_kT", [128, 16384], F32, kind="ExternalOutput").ap()
        dbg["vh"] = nc.dram_tensor("dbg_vh", [128, 16640], F32, kind="ExternalOutput").ap()
        dbg["a"] = nc.dram_tensor("dbg_a", [T, D], F32, kind="ExternalOutput").ap()
        dbg["x1"] = nc.dram_tensor("dbg_x1", [T, D], F32, kind="ExternalOutput").ap()
        dbg["h"] = nc.dram_tensor("dbg_h", [128, II], F32, kind="ExternalOutput").ap()
        dbg["aqh"] = nc.dram_tensor("dbg_aqh", [128, II], F32, kind="ExternalOutput").ap()
        dbg["xn1"] = nc.dram_tensor("dbg_xn1", [T, D], F32, kind="ExternalOutput").ap()
        dbg["qf"] = nc.dram_tensor("dbg_qf", [128, D], F32, kind="ExternalOutput").ap()
        dbg["ssq"] = nc.dram_tensor("dbg_ssq", [T, 16], F32, kind="ExternalOutput").ap()
        dbg["rn"] = nc.dram_tensor("dbg_rn", [T, 16], F32, kind="ExternalOutput").ap()
        dbg["dnm"] = nc.dram_tensor("dbg_dnm", [128, 256], F32, kind="ExternalOutput").ap()

    AR = nc.alloc_sbuf_tensor("arena", [128, 200 * KB], mybir.dt.uint8).ap()

    def fv(off, cols, p=128):
        return AR[0:p, off:off + 4 * cols].bitcast(F32)

    def bv(off, cols, p=128):
        return AR[0:p, off:off + 2 * cols].bitcast(BF16)

    def ivv(off, cols, p=128):
        return AR[0:p, off:off + 4 * cols].bitcast(I32)

    ones128 = bv(0, 128)
    g_b = fv(256, 1)
    negabsg = fv(320, 1)
    eps_b = fv(384, 1)
    ln127_b = fv(448, 1)
    m_b = {"g": fv(512, 1), "u": fv(576, 1), "d": fv(640, 1)}
    thr = {"g": fv(704, 1), "u": fv(768, 1), "d": fv(832, 1)}
    nthr = {"g": fv(896, 1), "u": fv(960, 1), "d": fv(1024, 1)}
    wsum = {"g": fv(1088, 1), "u": fv(1152, 1), "d": fv(1216, 1)}
    rsum = fv(1280, 1)
    betan = fv(1344, 16)
    dnm = fv(1472, 256)
    st = fv(2560, 16)
    ssq_q = fv(2688, 16)
    ssq_k = fv(2816, 16)
    inv_q = fv(2944, 16)
    inv_k = fv(3072, 16)
    tmp16 = fv(3200, 16)
    tmp16b = fv(3264, 16)
    rn16 = fv(3328, 16)
    amx = fv(3456, 1)
    amc = fv(3520, 1)
    sxv = fv(3584, 1)
    shv = fv(3648, 1)
    kgn = fv(3712, 1)
    kun = fv(3776, 1)
    sc1 = fv(3840, 1)
    sc2 = fv(3904, 1)
    rowb = fv(4096, 128, p=1)
    dtmp = fv(4608, 1, p=1)
    dval = fv(4672, 1, p=1)
    ident = bv(5120, 128)

    WT_O, KT_O, QT_O, VH_O = 8 * KB, 56 * KB, 88 * KB, 120 * KB
    PT_O, OT_O, OTT_O = 153 * KB, 165 * KB, 167 * KB
    XNT_O = 165 * KB
    WGT_O, WUT_O = 8 * KB, 72 * KB
    XN3_O = 136 * KB
    ATI_O = 140 * KB
    A2_O = 144 * KB
    SQG_O = 146 * KB
    GX1_O = 150 * KB
    ZI_O = 140 * KB
    FQ_O = 144 * KB
    T2_O = 148 * KB
    AQN_O = 152 * KB
    AQT_O = 154 * KB
    RG_O = 156 * KB
    HB_O = 164 * KB
    AQHB_O = 181 * KB
    AHT_O = 189 * KB
    WDT_O = 8 * KB
    AHI_O = 72 * KB
    LN4_O = 140 * KB
    OO_O = 144 * KB
    SQD_O = 148 * KB
    XT_O = 173 * KB
    TO_O = 181 * KB
    T1_O = 185 * KB
    SQ_O = 169 * KB
    QF_O = 153 * KB
    KF_O = 157 * KB
    XN_O = 161 * KB
    X2N_O = 163 * KB
    QH_O = 181 * KB
    KH_O = 185 * KB

    wqT, wkT, wvT = bv(WT_O, 8192), bv(WT_O + 16 * KB, 8192), bv(WT_O + 32 * KB, 8192)
    kT, qT = bv(KT_O, 16384), bv(QT_O, 16384)
    vh = bv(VH_O, 16640)
    ptil = bv(PT_O, 6144)
    oT = [bv(OT_O, 512), bv(OT_O + KB, 512)]
    oTT = [bv(OTT_O + 192 * i, 80) for i in range(16)]
    wgT, wuT, wdT = bv(WGT_O, 32768), bv(WUT_O, 32768), bv(WDT_O, 32768)

    psA = nc.alloc_psum_tensor("ps", [128, 4096], F32).ap()

    def PS(bank, n=512, p=128):
        return psA[0:p, 512 * bank:512 * bank + n]

    Sm = lambda name: Sem(nc, name)
    s_wld, s_wtp, s_g, s_cst = Sm("wld"), Sm("wtp"), Sm("sg"), Sm("cst")
    s_x, s_sq, s_st1, s_rst = Sm("sx"), Sm("ssq"), Sm("sst1"), Sm("srst")
    s_xn1, s_xnT, s_proj, s_ev = Sm("sxn1"), Sm("sxnT"), Sm("sproj"), Sm("sev")
    s_nrm, s_qk, s_qkT = Sm("snrm"), Sm("sqk"), Sm("sqkT")
    s_S, s_exp, s_avj = Sm("sS"), Sm("sexp"), Sm("savj")
    s_oT, s_oTT, s_dn, s_a = Sm("soT"), Sm("soTT"), Sm("sdn"), Sm("sa")
    s_wst, s_ws, s_tst, s_red, s_thr = Sm("swst"), Sm("sws"), Sm("stst"), Sm("sred"), Sm("sthr")
    s_ga, s_gx, s_rn, s_as, s_x1, s_x1st = Sm("sga"), Sm("sgx"), Sm("srn"), Sm("sas"), Sm("sx1"), Sm("sx1st")
    s_wgT, s_q1, s_f2, s_f3, s_aqT = Sm("swgT"), Sm("sq1"), Sm("sf2"), Sm("sf3"), Sm("saqT")
    s_gu, s_f1, s_hq, s_hx, s_hs, s_q2 = Sm("sgu"), Sm("sf1"), Sm("shq"), Sm("shx"), Sm("shs"), Sm("sq2")
    s_ahT, s_ahst, s_wdT = Sm("sahT"), Sm("sahst"), Sm("swdT")
    s_ahl, s_dmm, s_d1, s_d2, s_d3, s_d4, s_out = (
        Sm("sahl"), Sm("sdmm"), Sm("sd1"), Sm("sd2"), Sm("sd3"), Sm("sd4"), Sm("sout"))
    s_dbg = Sm("sdbg")
    s_xP = [Sm("sxA"), Sm("sxB")]
    s_gaP = [Sm("sgaA"), Sm("sgaB")]
    s_gxP = [Sm("sgxA"), Sm("sgxB")]
    s_wstP = [Sm("swstA"), Sm("swstB")]
    s_ahlP = [Sm("sahlA"), Sm("sahlB")]
    s_tstP = [Sm("ststA"), Sm("ststB")]
    s_xnTP = [Sm("sxnTA"), Sm("sxnTB")]
    s_wtp0 = Sm("swtp0")
    s_vf = Sm("svf")
    s_id = Sm("sid")
    s_ta, s_ea = Sm("sta"), Sm("sea")
    s_th, s_eh = Sm("sth"), Sm("seh")
    s_pet, s_pev = Sm("spet"), Sm("spev")

    def VF(e, instr):
        s_vf.inc(instr)
        e.wait_ge(s_vf.h, s_vf.n)
        return instr

    def SF(e, instr):
        s_vf.inc(instr)
        e.wait_ge(s_vf.h, s_vf.n)
        return instr

    with nc.Block() as blk:
        @blk.scalar
        def _(e):
            s_g.inc(e.dma_start(out=g_b, in_=g_d.broadcast_to([128, 1])), 16)

        @blk.vector
        def _(e):
            e.memset(ones128, 1.0)
            e.memset(eps_b, 1e-5)
            e.memset(ln127_b, LN127)
            e.memset(vh, 1.0)
            e.memset(wsum["g"], 0.0)
            e.memset(wsum["u"], 0.0)
            e.memset(wsum["d"], 0.0)
            e.wait_ge(s_g.h, 16)
            e.tensor_scalar(out=sc1, in0=g_b, scalar1=-1.0, scalar2=None,
                            op0=ALU.mult)
            s_cst.inc(e.tensor_tensor(out=negabsg, in0=sc1, in1=g_b, op=ALU.min))

        @blk.gpsimd
        def _(e):
            e.memset(ident, 0.0)
            s_id.inc(e.affine_select(
                out=ident, in_=ident, compare_op=ALU.not_equal, fill=1.0,
                base=0, pattern=[[-1, 128]], channel_multiplier=1))
            for wi, wsrc in enumerate([wq_d, wk_d, wv_d]):
                wT = [wqT, wkT, wvT][wi]
                for c in range(8):
                    s_wld.inc(e.dma_start(
                        out=wT[:, c * 1024:(c + 1) * 1024],
                        in_=wsrc[128 * c:128 * c + 128, :]), 16)

    with nc.Block() as blk:
        @blk.gpsimd
        def _(e):
            for t in range(NT):
                if t >= 2:
                    e.wait_ge(s_xn1.h, t - 1)
                s_xP[t % 2].inc(e.dma_start(out=fv(XT_O + (t % 2) * 4 * KB, 1024),
                                            in_=x_d[128 * t:128 * t + 128, :]), 16)
                if debug:
                    e.wait_ge(s_xn1.h, t + 1)
                    s_dbg.inc(e.dma_start(out=dbg["xn1"][128 * t:128 * t + 128, :],
                                          in_=bv(XN_O, 1024)), 16)
                    if t == 0:
                        e.wait_ge(s_ev.h, 1)
                        s_dbg.inc(e.dma_start(out=dbg["qf"], in_=fv(QF_O, 1024)), 16)
                    e.wait_ge(s_ev.h, t + 1)
                    s_dbg.inc(e.dma_start(out=dbg["ssq"][128 * t:128 * t + 128, :],
                                          in_=ssq_q), 16)
                    e.wait_ge(s_qk.h, t + 1)
                    s_dbg.inc(e.dma_start(out=dbg["rn"][128 * t:128 * t + 128, :],
                                          in_=rn16), 16)

        @blk.scalar
        def _(e):
            psAb8 = psA.bitcast(BF16)[:, 6144:7168].rearrange(
                "p (c w) -> p c w", c=8)

            def evac_qk(dst, tp, pet_target):
                e.wait_ge(s_pet.h, pet_target)
                s_pev.inc(e.activation(
                    out=dst.rearrange("p (c w) -> p c w", c=8)[:, :, 128 * tp:
                                                              128 * tp + 128],
                    in_=psAb8, func=ACTF.Copy))

            for t in range(NT):
                xt = fv(XT_O + (t % 2) * 4 * KB, 1024)
                sq = fv(SQ_O, 1024)
                e.wait_ge(s_xP[t % 2].h, 16 * (t // 2 + 1))
                if t >= 1:
                    e.wait_ge(s_ev.h, t)
                s_sq.inc(e.activation(out=sq, in_=xt, func=ACTF.Square,
                                      accum_out=st[:, 1:2]))
                e.wait_ge(s_st1.h, t + 1)
                e.activation(out=st[:, 6:7], in_=st[:, 5:6], func=ACTF.Ln, bias=eps_b)
                s_rst.inc(e.activation(out=st[:, 7:8], in_=st[:, 6:7], func=ACTF.Exp,
                                       scale=-0.5))
                if t >= 1:
                    evac_qk(qT, t - 1, 3 * t - 1)
                    evac_qk(kT, t - 1, 3 * t)
                e.wait_ge(s_pet.h, 3 * t + 1)
                s_pev.inc(e.activation(out=bv(XNT_O, 1024),
                                       in_=psA.bitcast(BF16)[:, 6144:7168],
                                       func=ACTF.Copy))
                e.wait_ge(s_ev.h, t + 1)
                e.activation(out=tmp16, in_=ssq_q, func=ACTF.Ln)
                e.activation(out=inv_q, in_=tmp16, func=ACTF.Exp, scale=-0.5)
                e.activation(out=tmp16b, in_=ssq_k, func=ACTF.Ln)
                s_nrm.inc(e.activation(out=inv_k, in_=tmp16b, func=ACTF.Exp, scale=-0.5))
            evac_qk(qT, NT - 1, 3 * NT - 1)
            evac_qk(kT, NT - 1, 3 * NT)

        @blk.vector
        def _(e):
            for t in range(NT):
                xt = fv(XT_O + (t % 2) * 4 * KB, 1024)
                e.wait_ge(s_xP[t % 2].h, 16 * (t // 2 + 1))
                VF(e, e.reduce_sum(out=st[:, 0:1], in_=xt, axis=AX.X))
                e.wait_ge(s_sq.h, t + 1)
                SF(e, e.tensor_scalar(out=st[:, 2:3], in0=st[:, 0:1],
                                      scalar1=1.0 / D, scalar2=None,
                                      op0=ALU.mult))
                VF(e, e.tensor_scalar(out=st[:, 3:4], in0=st[:, 1:2], scalar1=1.0 / D,
                                      scalar2=None, op0=ALU.mult))
                VF(e, e.tensor_mul(out=st[:, 4:5], in0=st[:, 2:3], in1=st[:, 2:3]))
                s_st1.inc(e.tensor_sub(out=st[:, 5:6], in0=st[:, 3:4], in1=st[:, 4:5]))
                e.wait_ge(s_rst.h, t + 1)
                if t >= 1:
                    e.wait_ge(s_pet.h, 3 * t - 2)
                xn1 = bv(XN_O, 1024)
                s_xn1.inc(e.tensor_scalar(out=xn1, in0=xt, scalar1=st[:, 2:3],
                                          scalar2=st[:, 7:8], op0=ALU.subtract,
                                          op1=ALU.mult))
                e.wait_ge(s_proj.h, t + 1)
                qf, kf, sq = fv(QF_O, 1024), fv(KF_O, 1024), fv(SQ_O, 1024)
                e.tensor_copy(out=qf[:, 0:512], in_=PS(0))
                e.tensor_copy(out=qf[:, 512:1024], in_=PS(1))
                e.tensor_copy(out=kf[:, 0:512], in_=PS(2))
                e.tensor_copy(out=kf[:, 512:1024], in_=PS(3))
                last = None
                for n in range(2):
                    last = e.tensor_copy(
                        out=vh[:, t * 1040 + 520 * n: t * 1040 + 520 * (n + 1)]
                        .rearrange("p (h d) -> p h d", h=8, d=65)[:, :, 0:64],
                        in_=PS(4 + n).rearrange("p (h d) -> p h d", h=8, d=64))
                VF(e, last)
                VF(e, e.tensor_mul(out=sq, in0=qf, in1=qf))
                VF(e, e.reduce_sum(out=ssq_q,
                                   in_=sq.rearrange("p (h d) -> p h d", d=64),
                                   axis=AX.X))
                e.tensor_scalar(out=ssq_q, in0=ssq_q, scalar1=1e-24, scalar2=None,
                                op0=ALU.max)
                VF(e, e.tensor_mul(out=sq, in0=kf, in1=kf))
                VF(e, e.reduce_sum(out=ssq_k,
                                   in_=sq.rearrange("p (h d) -> p h d", d=64),
                                   axis=AX.X))
                s_ev.inc(e.tensor_scalar(out=ssq_k, in0=ssq_k, scalar1=1e-24,
                                         scalar2=None, op0=ALU.max))
                e.wait_ge(s_nrm.h, t + 1)
                e.wait_ge(s_cst.h, 1)
                if t >= 2:
                    e.wait_ge(s_pet.h, 3 * (t - 1))
                if debug:
                    e.wait_ge(s_dbg.h, 16 * (t + 1 + (1 if t >= 0 else 0)))
                SF(e, e.tensor_scalar(out=rn16, in0=inv_q, scalar1=g_b,
                                      scalar2=None, op0=ALU.mult))
                qh = bv(QH_O + (t % 2) * 2 * KB, 1024)
                kh = bv(KH_O + (t % 2) * 2 * KB, 1024)
                rn3 = rn16.rearrange("p (h o) -> p h o", o=1).broadcast_to(
                    [128, 16, 64])
                ik3 = inv_k.rearrange("p (h o) -> p h o", o=1).broadcast_to(
                    [128, 16, 64])
                e.tensor_tensor(out=qh.rearrange("p (h d) -> p h d", d=64),
                                in0=qf.rearrange("p (h d) -> p h d", d=64),
                                in1=rn3, op=ALU.mult)
                s_qk.inc(e.tensor_tensor(
                    out=kh.rearrange("p (h d) -> p h d", d=64),
                    in0=kf.rearrange("p (h d) -> p h d", d=64),
                    in1=ik3, op=ALU.mult))

        @blk.tensor
        def _(e):
            e.wait_ge(s_wld.h, 16 * 24)
            e.wait_ge(s_id.h, 1)
            psAb = psA.bitcast(BF16)

            def qk_trans(e, tp):
                e.wait_ge(s_qk.h, tp + 1)
                e.wait_ge(s_pev.h, 3 * tp + 1)
                qh_p = bv(QH_O + (tp % 2) * 2 * KB, 1024)
                kh_p = bv(KH_O + (tp % 2) * 2 * KB, 1024)
                last = None
                for c in range(8):
                    last = e.transpose(
                        out=psAb[:, 6144 + 128 * c:6144 + 128 * (c + 1)],
                        in_=qh_p[:, 128 * c:128 * (c + 1)], identity=ident)
                s_pet.inc(last)
                e.wait_ge(s_pev.h, 3 * tp + 2)
                last = None
                for c in range(8):
                    last = e.transpose(
                        out=psAb[:, 6144 + 128 * c:6144 + 128 * (c + 1)],
                        in_=kh_p[:, 128 * c:128 * (c + 1)], identity=ident)
                s_pet.inc(last)

            for t in range(NT):
                if t >= 1:
                    qk_trans(e, t - 1)
                e.wait_ge(s_xn1.h, t + 1)
                if t >= 1:
                    e.wait_ge(s_pev.h, 3 * t)
                xn1 = bv(XN_O, 1024)
                last = None
                for c in range(8):
                    last = e.transpose(
                        out=psAb[:, 6144 + 128 * c:6144 + 128 * (c + 1)],
                        in_=xn1[:, 128 * c:128 * (c + 1)], identity=ident)
                s_pet.inc(last)
                e.wait_ge(s_pev.h, 3 * t + 1)
                if t >= 1:
                    e.wait_ge(s_ev.h, t)
                last = None
                for c in range(8):
                    xn1T_c = bv(XNT_O + 256 * c, 128)
                    for wi, wT in enumerate([wqT, wkT, wvT]):
                        for n in range(2):
                            last = e.matmul(
                                out=PS(2 * wi + n), lhsT=xn1T_c,
                                rhs=wT[:, c * 1024 + 512 * n: c * 1024 + 512 * (n + 1)],
                                start=(c == 0), stop=(c == 7))
                s_proj.inc(last)
            qk_trans(e, NT - 1)

    nc.all_engine_barrier()

    if debug:
        with nc.Block() as blk:
            @blk.gpsimd
            def _(e):
                s_dbg.inc(e.dma_start(out=dbg["qT"], in_=qT), 16)
                s_dbg.inc(e.dma_start(out=dbg["kT"], in_=kT), 16)
                s_dbg.inc(e.dma_start(out=dbg["vh"], in_=vh), 16)
            @blk.sync
            def _(e):
                e.wait_ge(s_dbg.h, 16 * 3)
        nc.all_engine_barrier()

    items_by_unit = [[] for _ in range(32)]

    def add_item(u, it):
        items_by_unit[u].append(it)

    gu_tiles = [("g", r) for r in range(32)] + [("u", r) for r in range(32)]
    for i, (w, r) in enumerate(gu_tiles[:32]):
        add_item(i // 8, ("p1", w, r))
    for i, (w, r) in enumerate(gu_tiles[32:]):
        add_item(4 + i // 8, ("p1", w, r))
    for i in range(32):
        add_item(8 + i // 4, ("p1d", i // 4, i % 4))
    add_item(8, ("dnc1", "g")); add_item(9, ("dnc2", "g"))
    add_item(10, ("dnc1", "u")); add_item(11, ("dnc2", "u"))
    add_item(16, ("dnc1", "d")); add_item(17, ("dnc2", "d"))
    for i, (w, r) in enumerate(gu_tiles[:32]):
        add_item(11 + i // 4, ("p2", w, r))
    for i, (w, r) in enumerate(gu_tiles[32:]):
        add_item(19 + i // 4, ("p2", w, r))
    for i in range(32):
        add_item(min(27 + i // 7, 31), ("p2d", i // 4, i % 4))

    all_items = [it for u in range(32) for it in items_by_unit[u]]
    unit_of_item = [u for u in range(32) for _ in items_by_unit[u]]
    cum_ws = []
    c = 0
    for u in range(32):
        c += len(items_by_unit[u])
        cum_ws.append(c)
    item_load_idx, item_store_idx = {}, {}
    load_gidx, store_gidx = [], []
    for gi, it in enumerate(all_items):
        if it[0] in ("p1", "p2", "p1d", "p2d"):
            item_load_idx[gi] = len(load_gidx)
            load_gidx.append(gi)
            if it[0] in ("p2", "p2d"):
                item_store_idx[gi] = len(store_gidx)
                store_gidx.append(gi)
    n_gu_stores = sum(1 for it in all_items if it[0] == "p2")

    def load_ap(it):
        if it[0] == "p1" or it[0] == "p2":
            w, r = it[1], it[2]
            kc, cc = r // 4, r % 4
            return {"g": wg_d, "u": wu_d}[w][128 * kc:128 * kc + 128,
                                             1024 * cc:1024 * cc + 1024]
        rc = 4 * it[1] + it[2]
        return wd_d[128 * rc:128 * rc + 128, :]

    def store_ap(it):
        if it[0] == "p2":
            w, r = it[1], it[2]
            kc, cc = r // 4, r % 4
            return {"g": tg_d, "u": tu_d}[w][128 * kc:128 * kc + 128,
                                             1024 * cc:1024 * cc + 1024]
        rc = 4 * it[1] + it[2]
        return td_d[128 * rc:128 * rc + 128, :]

    drank = {"g": 0, "u": 1, "d": 2}
    p1_done_ws = {"g": cum_ws[3], "u": cum_ws[7], "d": cum_ws[15]}

    with nc.Block() as blk:
        @blk.tensor
        def _(e):
            e.wait_ge(s_pev.h, 3 * NT)
            for u in range(32):
                hp, ic = u // 4, u % 4

                def av(j, stop):
                    last = None
                    sl = bv(PT_O + 2048 * (j % 6), 1024)
                    for hh in range(2):
                        vcol = j * 1040 + 65 * (2 * hp + hh)
                        last = e.matmul(
                            out=PS(6 + hh, 512, p=65),
                            lhsT=vh[:, vcol:vcol + 65],
                            rhs=sl[:, 512 * hh:512 * hh + 512],
                            start=(j == 0), stop=stop)
                    s_avj.inc(last)

                def score(j):
                    p = j % 3
                    kcol = hp * 2048 + 128 * j
                    qcol = hp * 2048 + 512 * ic
                    s_S.inc(e.matmul(out=psA[:, 1024 * p:1024 * p + 512],
                                     lhsT=kT[0:64, kcol:kcol + 128],
                                     rhs=qT[0:64, qcol:qcol + 512],
                                     start=True, stop=True,
                                     tile_position=(0, 0)))
                    s_S.inc(e.matmul(out=psA[:, 1024 * p + 512:1024 * (p + 1)],
                                     lhsT=kT[64:128, kcol:kcol + 128],
                                     rhs=qT[64:128, qcol:qcol + 512],
                                     start=True, stop=True,
                                     tile_position=(64, 0)))

                for b in range(6):
                    js = list(range(3 * b, min(3 * b + 3, 16)))
                    if b == 0:
                        e.wait_ge(s_oT.h, 2 * u)
                        if u >= 1:
                            e.wait_ge(s_exp.h, 16 * u)
                    else:
                        e.wait_ge(s_exp.h, 16 * u + 3 * b)
                    for j in js:
                        score(j)
                    for j in range(max(0, 3 * b - 3), 3 * b):
                        av(j, stop=False)
                e.wait_ge(s_exp.h, 16 * u + 16)
                av(15, stop=True)

        @blk.scalar
        def _(e):
            for u in range(32):
                for j in range(16):
                    e.wait_ge(s_S.h, 32 * u + 2 * (j + 1))
                    if 16 * u + j - 5 > 0:
                        e.wait_ge(s_avj.h, 16 * u + j - 5)
                    p = j % 3
                    s_exp.inc(e.activation(
                        out=bv(PT_O + 2048 * (j % 6), 1024),
                        in_=psA[:, 1024 * p:1024 * (p + 1)], func=ACTF.Exp))

        @blk.sync
        def _(e):
            for u in range(32):
                e.wait_ge(s_oT.h, 2 * (u + 1))
                if u >= 2:
                    e.wait_ge(s_dn.h, u - 1)
                    e.wait_ge(s_a.h, 128 * (u - 1))
                for hh in range(2):
                    for b in range(4):
                        s_oTT.inc(e.dma_start_transpose(
                            out=oTT[8 * (u % 2) + 4 * hh + b],
                            in_=oT[hh][0:80, 128 * b:128 * b + 128]), 16)
                e.wait_ge(s_oTT.h, 128 * (u + 1))
                for hh in range(2):
                    for b in range(4):
                        rows = (u % 4) * 512 + 128 * b
                        s_a.inc(e.dma_start(
                            out=a_d[rows:rows + 128,
                                    64 * (2 * (u // 4) + hh):
                                    64 * (2 * (u // 4) + hh) + 64],
                            in_=oTT[8 * (u % 2) + 4 * hh + b][:, 0:64]), 16)

        @blk.vector
        def _(e):
            gidx = 0
            for u in range(32):
                hp, ic = u // 4, u % 4
                e.wait_ge(s_avj.h, 16 * (u + 1))
                s_oT.inc(e.tensor_copy(out=oT[0][0:65, :], in_=PS(6, p=65)))
                s_oT.inc(e.tensor_copy(out=oT[1][0:65, :], in_=PS(7, p=65)))
                for it in items_by_unit[u]:
                    if it[0] in ("p1", "p1d"):
                        k = item_load_idx[gidx]
                        wt = fv(XT_O + (k % 2) * 4 * KB, 1024)
                        wname = it[1] if it[0] == "p1" else "d"
                        e.wait_ge(s_wstP[k % 2].h, 16 * (k // 2 + 1))
                        VF(e, e.reduce_sum(out=rsum, in_=wt, axis=AX.X,
                                           apply_absolute_value=True))
                        s_ws.inc(e.tensor_add(out=wsum[wname], in0=wsum[wname],
                                              in1=rsum))
                    elif it[0] in ("p2", "p2d"):
                        k = item_load_idx[gidx]
                        sidx = item_store_idx[gidx]
                        wt = fv(XT_O + (k % 2) * 4 * KB, 1024)
                        wname = it[1] if it[0] == "p2" else "d"
                        e.wait_ge(s_wstP[k % 2].h, 16 * (k // 2 + 1))
                        e.wait_ge(s_thr.h, {"g": 1, "u": 2, "d": 3}[wname])
                        if sidx >= 2:
                            e.wait_ge(s_tstP[sidx % 2].h, 16 * (sidx // 2))
                        t1 = fv(T1_O, 1024)
                        to = bv(TO_O + (sidx % 2) * 2 * KB, 1024)
                        e.tensor_scalar(out=t1, in0=wt, scalar1=thr[wname],
                                        scalar2=None, op0=ALU.is_ge)
                        VF(e, e.tensor_scalar(out=to, in0=wt, scalar1=nthr[wname],
                                              scalar2=None, op0=ALU.is_le))
                        s_ws.inc(e.tensor_sub(out=to, in0=t1, in1=to))
                    elif it[0] == "dnc1":
                        rk = drank[it[1]]
                        e.wait_ge(s_red.h, 64 * rk + 32)
                        VF(e, e.reduce_sum(out=dtmp, in_=rowb, axis=AX.X))
                        s_ws.inc(e.tensor_scalar(out=dval, in0=dtmp, scalar1=NMEAN,
                                                 scalar2=1e-5, op0=ALU.mult,
                                                 op1=ALU.max))
                    elif it[0] == "dnc2":
                        w = it[1]
                        rk = drank[w]
                        e.wait_ge(s_red.h, 64 * rk + 64)
                        SF(e, e.tensor_scalar(out=thr[w], in0=m_b[w], scalar1=0.5,
                                              scalar2=None, op0=ALU.mult))
                        SF(e, e.tensor_scalar(out=nthr[w], in0=thr[w], scalar1=-1.0,
                                              scalar2=None, op0=ALU.mult))
                        VF(e, e.tensor_scalar(out=sc1, in0=nthr[w], scalar1=1.0,
                                               scalar2=None, op0=ALU.mult))
                        s_thr.inc(e.tensor_copy(out=sc1, in_=thr[w]))
                        s_ws.inc(e.tensor_copy(out=sc2, in_=thr[w]))
                    gidx += 1
                e.wait_ge(s_oTT.h, 128 * (u + 1))
                last = None
                for hh in range(2):
                    for b in range(4):
                        tt = ic * 4 + b
                        col = tt * 16 + 2 * hp + hh
                        last = e.tensor_copy(out=dnm[:, col:col + 1],
                                             in_=oTT[8 * (u % 2) + 4 * hh + b][:, 64:65])
                s_dn.inc(last)

        @blk.gpsimd
        def _(e):
            gidx = 0
            for u in range(32):
                for it in items_by_unit[u]:
                    if it[0] == "dnc1":
                        w = it[1]
                        rk = drank[w]
                        e.wait_ge(s_ws.h, p1_done_ws[w])
                        s_red.inc(e.dma_start(out=red_d[2 * rk:2 * rk + 1, :],
                                              in_=wsum[w]), 16)
                        e.wait_ge(s_red.h, 64 * rk + 16)
                        s_red.inc(e.dma_start(out=rowb,
                                              in_=red_d[2 * rk:2 * rk + 1, :]), 16)
                    elif it[0] == "dnc2":
                        w = it[1]
                        rk = drank[w]
                        e.wait_ge(s_ws.h, cum_ws[u - 1])
                        s_red.inc(e.dma_start(out=red_d[2 * rk + 1:2 * rk + 2, 0:1],
                                              in_=dval), 16)
                        e.wait_ge(s_red.h, 64 * rk + 48)
                        s_red.inc(e.dma_start(
                            out=m_b[w],
                            in_=red_d[2 * rk + 1:2 * rk + 2, 0:1]
                            .broadcast_to([128, 1])), 16)
                    if it[0] in ("p1", "p2", "p1d", "p2d"):
                        k = item_load_idx[gidx]
                        if k >= 2:
                            e.wait_ge(s_ws.h, load_gidx[k - 2] + 1)
                        s_wstP[k % 2].inc(
                            e.dma_start(out=fv(XT_O + (k % 2) * 4 * KB, 1024),
                                        in_=load_ap(it)), 16)
                    if gidx in item_store_idx:
                        si = item_store_idx[gidx]
                        e.wait_ge(s_ws.h, gidx + 1)
                        s_tstP[si % 2].inc(e.dma_start(
                            out=store_ap(it),
                            in_=bv(TO_O + (si % 2) * 2 * KB, 1024)), 16)
                    gidx += 1

    nc.all_engine_barrier()

    b_sq, b_st1, b_rst, b_x = s_sq.n, s_st1.n, s_rst.n, s_x.n
    with nc.Block() as blk:
        @blk.sync
        def _(e):
            ng0 = (n_gu_stores + 1) // 2
            ng1 = n_gu_stores // 2
            e.wait_ge(s_tstP[0].h, 16 * ng0)
            e.wait_ge(s_tstP[1].h, 16 * ng1)
            for wT, src in [(wgT, tg_d), (wuT, tu_d)]:
                for cc in range(8):
                    s_wgT.inc(e.dma_start(
                        out=wT[:, cc * 4096:(cc + 1) * 4096],
                        in_=src[128 * cc:128 * cc + 128, :]), 16)

        @blk.gpsimd
        def _(e):
            e.wait_ge(s_a.h, 128 * 32)
            if debug:
                s_dbg.inc(e.dma_start(out=dbg["dnm"], in_=dnm), 16)
            for t in range(NT):
                if t >= 2:
                    e.wait_ge(s_as.h, t - 1)
                s_gaP[t % 2].inc(e.dma_start(out=bv(ATI_O + (t % 2) * 2 * KB, 1024),
                                             in_=a_d[128 * t:128 * t + 128, :]), 16)
                e.wait_ge(s_x1.h, t + 1)
                s_x1st.inc(e.dma_start(out=x1_d[128 * t:128 * t + 128, :],
                                       in_=fv(GX1_O, 1024)), 16)
                if debug:
                    s_dbg.inc(e.dma_start(out=dbg["x1"][128 * t:128 * t + 128, :],
                                          in_=fv(GX1_O, 1024)), 16)
                    e.wait_ge(s_as.h, t + 1)
                    s_dbg.inc(e.dma_start(out=dbg["a"][128 * t:128 * t + 128, :],
                                          in_=bv(A2_O, 1024)), 16)

        @blk.scalar
        def _(e):
            for t in range(NT):
                if t >= 1:
                    e.wait_ge(s_x1.h, t)
                s_gxP[t % 2].inc(e.dma_start(out=fv(XT_O + (t % 2) * 4 * KB, 1024),
                                             in_=x_d[128 * t:128 * t + 128, :]), 16)
                e.wait_ge(s_gaP[t % 2].h, 16 * (t // 2 + 1))
                e.activation(out=tmp16, in_=dnm[:, 16 * t:16 * t + 16], func=ACTF.Ln)
                s_rn.inc(e.activation(out=rn16, in_=tmp16, func=ACTF.Exp, scale=-1.0))
                e.wait_ge(s_as.h, t + 1)
                a2 = bv(A2_O, 1024)
                sq = fv(SQG_O, 1024)
                s_sq.inc(e.activation(out=sq, in_=a2, func=ACTF.Square,
                                      accum_out=st[:, 1:2]))
                e.wait_ge(s_st1.h, b_st1 + t + 1)
                e.activation(out=st[:, 6:7], in_=st[:, 5:6], func=ACTF.Ln, bias=eps_b)
                s_rst.inc(e.activation(out=st[:, 7:8], in_=st[:, 6:7], func=ACTF.Exp,
                                       scale=-0.5))

        @blk.vector
        def _(e):
            for t in range(NT):
                e.wait_ge(s_rn.h, t + 1)
                if t >= 1:
                    e.wait_ge(s_sq.h, b_sq + t)
                at = bv(ATI_O + (t % 2) * 2 * KB, 1024)
                a2 = bv(A2_O, 1024)
                rn3 = rn16.rearrange("p (h o) -> p h o", o=1).broadcast_to(
                    [128, 16, 64])
                s_as.inc(e.tensor_tensor(
                    out=a2.rearrange("p (h d) -> p h d", d=64),
                    in0=at.rearrange("p (h d) -> p h d", d=64),
                    in1=rn3, op=ALU.mult))
                e.wait_ge(s_as.h, s_as.n)
                VF(e, e.reduce_sum(out=st[:, 0:1], in_=a2, axis=AX.X))
                e.wait_ge(s_sq.h, b_sq + t + 1)
                SF(e, e.tensor_scalar(out=st[:, 2:3], in0=st[:, 0:1],
                                      scalar1=1.0 / D, scalar2=None,
                                      op0=ALU.mult))
                VF(e, e.tensor_scalar(out=st[:, 3:4], in0=st[:, 1:2],
                                      scalar1=1.0 / D, scalar2=None, op0=ALU.mult))
                VF(e, e.tensor_mul(out=st[:, 4:5], in0=st[:, 2:3], in1=st[:, 2:3]))
                s_st1.inc(e.tensor_sub(out=st[:, 5:6], in0=st[:, 3:4], in1=st[:, 4:5]))
                e.wait_ge(s_rst.h, b_rst + t + 1)
                ln2 = fv(XN3_O, 1024)
                VF(e, e.tensor_scalar(out=ln2, in0=a2, scalar1=st[:, 2:3],
                                      scalar2=st[:, 7:8], op0=ALU.subtract,
                                      op1=ALU.mult))
                e.wait_ge(s_gxP[t % 2].h, 16 * (t // 2 + 1))
                if t >= 1:
                    e.wait_ge(s_x1st.h, 16 * t)
                xt = fv(XT_O + (t % 2) * 4 * KB, 1024)
                s_x1.inc(e.tensor_add(out=fv(GX1_O, 1024), in0=xt, in1=ln2))

    nc.all_engine_barrier()

    b_sq, b_st1, b_rst = s_sq.n, s_st1.n, s_rst.n
    bxp = [s_xP[0].n, s_xP[1].n]
    with nc.Block() as blk:
        @blk.gpsimd
        def _(e):
            e.wait_ge(s_x1st.h, 16 * NT)
            for t in range(NT):
                if t >= 2:
                    e.wait_ge(s_f2.h, t - 1)
                s_xP[t % 2].inc(e.dma_start(out=fv(XT_O + (t % 2) * 4 * KB, 1024),
                                            in_=x1_d[128 * t:128 * t + 128, :]), 16)
                e.wait_ge(s_eh.h, 4 * (t + 1))
                s_ahst.inc(e.dma_start(
                    out=ahT_d[128 * t:128 * t + 128, :],
                    in_=bv(AHT_O, 4096)), 16)
                if debug and t == 0:
                    e.wait_ge(s_hx.h, 1)
                    s_dbg.inc(e.dma_start(out=dbg["h"], in_=bv(HB_O, 4096)), 16)
                    e.wait_ge(s_q2.h, 4)
                    s_dbg.inc(e.dma_start(out=dbg["aqh"], in_=bv(AQHB_O, 4096)), 16)

        @blk.scalar
        def _(e):
            for t in range(NT):
                xt = fv(XT_O + (t % 2) * 4 * KB, 1024)
                sq = fv(FQ_O, 1024)
                e.wait_ge(s_xP[t % 2].h, bxp[t % 2] + 16 * (t // 2 + 1))
                if t >= 1:
                    e.wait_ge(s_q2.h, 4 * t)
                s_sq.inc(e.activation(out=sq, in_=xt, func=ACTF.Square,
                                      accum_out=st[:, 1:2]))
                e.wait_ge(s_st1.h, b_st1 + t + 1)
                e.activation(out=st[:, 6:7], in_=st[:, 5:6], func=ACTF.Ln, bias=eps_b)
                s_rst.inc(e.activation(out=st[:, 7:8], in_=st[:, 6:7], func=ACTF.Exp,
                                       scale=-0.5))
                e.wait_ge(s_f2.h, t + 1)
                e.activation(out=sc2, in_=amc, func=ACTF.Ln)
                s_f3.inc(e.activation(out=sxv, in_=sc2, func=ACTF.Exp, scale=-1.0,
                                      bias=ln127_b))
                e.wait_ge(s_ta.h, t + 1)
                s_ea.inc(e.activation(out=bv(AQT_O, 1024),
                                      in_=psA.bitcast(BF16)[:, 4096:5120],
                                      func=ACTF.Copy))
                e.wait_ge(s_q1.h, t + 1)
                for q in range(4):
                    e.wait_ge(s_gu.h, 4 * t + q + 1)
                    for n in range(2):
                        idx = 2 * q + n
                        if idx >= 2:
                            e.wait_ge(s_hq.h, 8 * t + idx - 1)
                        rg = fv(RG_O + (idx % 2) * 2 * KB, 512)
                        rg2 = fv(RG_O + 4 * KB + (idx % 2) * 2 * KB, 512)
                        e.activation(out=rg, in_=PS(n), func=ACTF.Relu, scale=kgn)
                        s_f1.inc(e.activation(out=rg2, in_=rg, func=ACTF.Square))
                e.wait_ge(s_hx.h, t + 1)
                e.activation(out=sc2, in_=amc, func=ACTF.Ln)
                s_hs.inc(e.activation(out=shv, in_=sc2, func=ACTF.Exp, scale=-1.0,
                                      bias=ln127_b))
                for r in range(4):
                    e.wait_ge(s_th.h, 4 * t + r + 1)
                    if r == 0 and t >= 1:
                        e.wait_ge(s_ahst.h, 16 * t)
                    s_eh.inc(e.activation(out=bv(AHT_O + 2048 * r, 1024),
                                          in_=psA.bitcast(BF16)[:, 6144:7168],
                                          func=ACTF.Copy))

        @blk.vector
        def _(e):
            for t in range(NT):
                xt = fv(XT_O + (t % 2) * 4 * KB, 1024)
                e.wait_ge(s_xP[t % 2].h, bxp[t % 2] + 16 * (t // 2 + 1))
                VF(e, e.reduce_sum(out=st[:, 0:1], in_=xt, axis=AX.X))
                e.wait_ge(s_sq.h, b_sq + t + 1)
                SF(e, e.tensor_scalar(out=st[:, 2:3], in0=st[:, 0:1],
                                      scalar1=1.0 / D, scalar2=None,
                                      op0=ALU.mult))
                VF(e, e.tensor_scalar(out=st[:, 3:4], in0=st[:, 1:2],
                                      scalar1=1.0 / D, scalar2=None, op0=ALU.mult))
                VF(e, e.tensor_mul(out=st[:, 4:5], in0=st[:, 2:3], in1=st[:, 2:3]))
                s_st1.inc(e.tensor_sub(out=st[:, 5:6], in0=st[:, 3:4], in1=st[:, 4:5]))
                e.wait_ge(s_rst.h, b_rst + t + 1)
                xn3 = fv(XN3_O, 1024)
                VF(e, e.tensor_scalar(out=xn3, in0=xt, scalar1=st[:, 2:3],
                                      scalar2=st[:, 7:8], op0=ALU.subtract,
                                      op1=ALU.mult))
                VF(e, e.reduce_max(out=amx, in_=xn3, axis=AX.X,
                                   apply_absolute_value=True))
                s_f2.inc(e.tensor_scalar(out=amc, in0=amx, scalar1=1e-5,
                                         scalar2=None, op0=ALU.max))
                e.wait_ge(s_f3.h, t + 1)
                if t >= 1:
                    e.wait_ge(s_ta.h, t)
                zi = ivv(ZI_O, 1024)
                fq = fv(FQ_O, 1024)
                t2 = fv(T2_O, 1024)
                aqn = bv(AQN_O, 1024)
                VF(e, e.tensor_scalar(out=zi, in0=xn3, scalar1=sxv, scalar2=None,
                                      op0=ALU.mult))
                VF(e, e.scalar_tensor_tensor(out=fq, in0=xn3, scalar=sxv, in1=zi,
                                             op0=ALU.mult, op1=ALU.subtract))
                VF(e, e.scalar_tensor_tensor(out=t2, in0=fq, scalar=0.5, in1=zi,
                                             op0=ALU.is_gt, op1=ALU.add))
                e.scalar_tensor_tensor(out=aqn, in0=fq, scalar=-0.5, in1=t2,
                                       op0=ALU.is_lt, op1=ALU.subtract)
                e.scalar_tensor_tensor(out=kgn, in0=amc, scalar=-1.0 / 127.0,
                                       in1=m_b["g"], op0=ALU.mult, op1=ALU.mult)
                s_q1.inc(e.scalar_tensor_tensor(out=kun, in0=amc,
                                                scalar=-1.0 / 127.0,
                                                in1=m_b["u"], op0=ALU.mult,
                                                op1=ALU.mult))
                hb = bv(HB_O, 4096)
                for q in range(4):
                    for n in range(2):
                        idx = 2 * q + n
                        e.wait_ge(s_f1.h, 8 * t + idx + 1)
                        rg2 = fv(RG_O + 4 * KB + (idx % 2) * 2 * KB, 512)
                        s_hq.inc(e.scalar_tensor_tensor(
                            out=hb[:, 512 * idx:512 * (idx + 1)],
                            in0=PS(2 + n), scalar=kun, in1=rg2,
                            op0=ALU.mult, op1=ALU.mult))
                e.wait_ge(s_hq.h, s_hq.n)
                VF(e, e.reduce_max(out=amx, in_=hb, axis=AX.X,
                                   apply_absolute_value=True))
                VF(e, e.tensor_scalar(out=amc, in0=amx, scalar1=1e-5, scalar2=None,
                                      op0=ALU.max))
                s_hx.inc(e.scalar_tensor_tensor(out=betan[:, t:t + 1], in0=amc,
                                                scalar=-1.0 / 127.0, in1=m_b["d"],
                                                op0=ALU.mult, op1=ALU.mult))
                e.wait_ge(s_hs.h, t + 1)
                if t >= 1:
                    e.wait_ge(s_th.h, 4 * t)
                aqh = bv(AQHB_O, 4096)
                for half in range(4):
                    sl = slice(1024 * half, 1024 * (half + 1))
                    zi8 = ivv(ZI_O, 1024)
                    fq8 = fv(FQ_O, 1024)
                    t28 = fv(T2_O, 1024)
                    VF(e, e.tensor_scalar(out=zi8, in0=hb[:, sl], scalar1=shv,
                                          scalar2=None, op0=ALU.mult))
                    VF(e, e.scalar_tensor_tensor(out=fq8, in0=hb[:, sl], scalar=shv,
                                                 in1=zi8, op0=ALU.mult,
                                                 op1=ALU.subtract))
                    VF(e, e.scalar_tensor_tensor(out=t28, in0=fq8, scalar=0.5,
                                                 in1=zi8, op0=ALU.is_gt,
                                                 op1=ALU.add))
                    s_q2.inc(e.scalar_tensor_tensor(out=aqh[:, sl], in0=fq8,
                                                    scalar=-0.5, in1=t28,
                                                    op0=ALU.is_lt,
                                                    op1=ALU.subtract))

        @blk.tensor
        def _(e):
            e.wait_ge(s_wgT.h, 16 * 16)
            e.wait_ge(s_id.h, 1)
            for t in range(NT):
                e.wait_ge(s_q1.h, t + 1)
                if t >= 1:
                    e.wait_ge(s_ea.h, t)
                aqn = bv(AQN_O, 1024)
                psAb = psA.bitcast(BF16)
                last = None
                for c in range(8):
                    last = e.transpose(
                        out=psAb[:, 4096 + 128 * c:4096 + 128 * (c + 1)],
                        in_=aqn[:, 128 * c:128 * (c + 1)], identity=ident)
                s_ta.inc(last)
                e.wait_ge(s_ea.h, t + 1)
                for q in range(4):
                    if t > 0 or q > 0:
                        e.wait_ge(s_f1.h, 8 * t + 2 * q)
                        e.wait_ge(s_hq.h, 8 * t + 2 * q)
                    last = None
                    for cc in range(8):
                        aqT_c = bv(AQT_O + 256 * cc, 128)
                        for wi, wT in enumerate([wgT, wuT]):
                            for n in range(2):
                                col = cc * 4096 + 1024 * q + 512 * n
                                last = e.matmul(out=PS(2 * wi + n), lhsT=aqT_c,
                                                rhs=wT[:, col:col + 512],
                                                start=(cc == 0), stop=(cc == 7))
                    s_gu.inc(last)
                aqh_t = bv(AQHB_O, 4096)
                for r in range(4):
                    e.wait_ge(s_q2.h, 4 * t + r + 1)
                    if t > 0 or r > 0:
                        e.wait_ge(s_eh.h, 4 * t + r)
                    last = None
                    for c in range(8):
                        last = e.transpose(
                            out=psAb[:, 6144 + 128 * c:6144 + 128 * (c + 1)],
                            in_=aqh_t[:, 1024 * r + 128 * c:1024 * r + 128 * (c + 1)],
                            identity=ident)
                    s_th.inc(last)

    nc.all_engine_barrier()

    with nc.Block() as blk:
        @blk.sync
        def _(e):
            e.wait_ge(s_tstP[0].h, 16 * 48)
            e.wait_ge(s_tstP[1].h, 16 * 48)
            for kk in range(32):
                s_wdT.inc(e.dma_start(
                    out=wdT[:, kk * 1024:(kk + 1) * 1024],
                    in_=td_d[128 * kk:128 * kk + 128, :]), 16)
            for t in range(NT):
                e.wait_ge(s_d4.h, t + 1)
                s_out.inc(e.dma_start(out=out_d[128 * t:128 * t + 128, :],
                                      in_=fv(OO_O, 1024)), 16)
            e.wait_ge(s_out.h, 16 * NT)
            if debug:
                e.wait_ge(s_dbg.h, s_dbg.n)

        @blk.gpsimd
        def _(e):
            e.wait_ge(s_ahst.h, 16 * NT)
            e.wait_ge(s_x1st.h, 16 * NT)
            for t in range(NT):
                if t >= 2:
                    e.wait_ge(s_dmm.h, t - 1)
                s_ahlP[t % 2].inc(e.dma_start(out=bv(AHI_O + (t % 2) * 8 * KB, 4096),
                                              in_=ahT_d[128 * t:128 * t + 128, :]), 16)
                if t >= 2:
                    e.wait_ge(s_d4.h, t - 1)
                s_xP[t % 2].inc(e.dma_start(out=fv(XT_O + (t % 2) * 4 * KB, 1024),
                                            in_=x1_d[128 * t:128 * t + 128, :]), 16)

        @blk.tensor
        def _(e):
            e.wait_ge(s_wdT.h, 16 * 32)
            for t in range(NT):
                e.wait_ge(s_ahlP[t % 2].h, 16 * (t // 2 + 1))
                if t >= 1:
                    e.wait_ge(s_d1.h, t)
                ahi = bv(AHI_O + (t % 2) * 8 * KB, 4096)
                last = None
                for kk in range(32):
                    for n in range(2):
                        last = e.matmul(out=PS(n), lhsT=ahi[:, 128 * kk:128 * kk + 128],
                                        rhs=wdT[:, kk * 1024 + 512 * n:
                                                kk * 1024 + 512 * n + 512],
                                        start=(kk == 0), stop=(kk == 31))
                s_dmm.inc(last)

        @blk.scalar
        def _(e):
            for t in range(NT):
                e.wait_ge(s_dmm.h, t + 1)
                if t >= 1:
                    e.wait_ge(s_d4.h, t)
                dd = fv(XN3_O, 1024)
                sq = fv(SQD_O, 1024)
                e.activation(out=dd[:, 0:512], in_=PS(0), func=ACTF.Copy,
                             scale=betan[:, t:t + 1])
                e.activation(out=dd[:, 512:1024], in_=PS(1), func=ACTF.Copy,
                             scale=betan[:, t:t + 1])
                s_d1.inc(e.activation(out=sq, in_=dd, func=ACTF.Square,
                                      accum_out=st[:, 1:2]))
                e.wait_ge(s_d2.h, t + 1)
                e.activation(out=st[:, 6:7], in_=st[:, 5:6], func=ACTF.Ln, bias=eps_b)
                s_d3.inc(e.activation(out=st[:, 7:8], in_=st[:, 6:7], func=ACTF.Exp,
                                      scale=-0.5))

        @blk.vector
        def _(e):
            bx2p = [s_xP[0].n - 16 * 8, s_xP[1].n - 16 * 8]
            for t in range(NT):
                dd = fv(XN3_O, 1024)
                e.wait_ge(s_d1.h, t + 1)
                VF(e, e.reduce_sum(out=st[:, 0:1], in_=dd, axis=AX.X))
                SF(e, e.tensor_scalar(out=st[:, 2:3], in0=st[:, 0:1],
                                      scalar1=1.0 / D, scalar2=None,
                                      op0=ALU.mult))
                VF(e, e.tensor_scalar(out=st[:, 3:4], in0=st[:, 1:2],
                                      scalar1=1.0 / D, scalar2=None, op0=ALU.mult))
                VF(e, e.tensor_mul(out=st[:, 4:5], in0=st[:, 2:3], in1=st[:, 2:3]))
                s_d2.inc(e.tensor_sub(out=st[:, 5:6], in0=st[:, 3:4],
                                      in1=st[:, 4:5]))
                e.wait_ge(s_d3.h, t + 1)
                ln4 = fv(LN4_O, 1024)
                VF(e, e.tensor_scalar(out=ln4, in0=dd, scalar1=st[:, 2:3],
                                      scalar2=st[:, 7:8], op0=ALU.subtract,
                                      op1=ALU.mult))
                e.wait_ge(s_xP[t % 2].h, bx2p[t % 2] + 16 * (t // 2 + 1))
                if t >= 1:
                    e.wait_ge(s_out.h, 16 * t)
                xt = fv(XT_O + (t % 2) * 4 * KB, 1024)
                s_d4.inc(e.tensor_add(out=fv(OO_O, 1024), in0=xt, in1=ln4))

    return nc, dbg


def kernel(L0_attn_x, L0_attn_Wq, L0_attn_Wk, L0_attn_Wv, L0_attn_g_scale,
           L0_ffn_w_gate, L0_ffn_w_up, L0_ffn_w_down):
    nc, _ = build_nc(debug=False)
    x = np.ascontiguousarray(np.asarray(L0_attn_x, dtype=np.float32))
    g = np.asarray(L0_attn_g_scale, dtype=np.float32).reshape(1, 1)
    com = {
        "wq": np.ascontiguousarray(np.asarray(L0_attn_Wq, np.float32).T),
        "wk": np.ascontiguousarray(np.asarray(L0_attn_Wk, np.float32).T),
        "wv": np.ascontiguousarray(np.asarray(L0_attn_Wv, np.float32).T),
        "g": g,
        "wg": np.ascontiguousarray(np.asarray(L0_ffn_w_gate, np.float32).T),
        "wu": np.ascontiguousarray(np.asarray(L0_ffn_w_up, np.float32).T),
        "wd": np.ascontiguousarray(np.asarray(L0_ffn_w_down, np.float32).T),
    }
    in_maps = [dict(com, x=np.ascontiguousarray(x[i])) for i in range(8)]
    res = run_bass_kernel_spmd(nc, in_maps, core_ids=list(range(8)))
    return np.stack([np.asarray(r["out"], np.float32) for r in res.results], axis=0)
